# version 01
# baseline (speedup 1.0000x reference)
import numpy as np
import ml_dtypes
import concourse.bass as bass
import concourse.mybir as mybir
from concourse.bass_utils import run_bass_kernel_spmd
from contextlib import ExitStack

F32 = mybir.dt.float32
BF16 = mybir.dt.bfloat16
I32 = mybir.dt.int32
AF = mybir.ActivationFunctionType
OP = mybir.AluOpType
BF = ml_dtypes.bfloat16

D = 1024
H = 16
DH = 64
NF = 8
CH = 1024
FF = 4096
RANK = 64
NE = 8
NPAIR = 4
RMS_EPS = 1e-6
LN_EPS = 1e-5

SCAN_GROUPS = []
SCAN_GROUPS.append(("real", 0, 0, "zero"))
SCAN_GROUPS.append(("real", 0, 1, "carry"))
SCAN_GROUPS.append(("real", 1, 0, "carry"))
SCAN_GROUPS.append(("real", 1, 1, "carry"))
SCAN_GROUPS.append(("pfx2", 2, 0, "zero"))
SCAN_GROUPS.append(("pfx2", 2, 1, "carry"))
SCAN_GROUPS.append(("real", 2, 0, "carry"))
SCAN_GROUPS.append(("real", 2, 1, "carry"))
SCAN_GROUPS.append(("pfx3", 3, 0, "zero"))
SCAN_GROUPS.append(("pfx3", 3, 1, "carry"))
SCAN_GROUPS.append(("real", 3, 0, "carry"))
SCAN_GROUPS.append(("real", 3, 1, "carry"))


def build_program(debug_names=(), phases=3):
    nc = bass.Bass("TRN2", target_bir_lowering=False,
                   detect_race_conditions=False)

    dx = nc.dram_tensor("x", [4, CH, D], F32, kind="ExternalInput").ap()
    dxc2 = nc.dram_tensor("xc2", [CH, D], F32, kind="ExternalInput").ap()
    dxc3 = nc.dram_tensor("xc3", [CH, D], F32, kind="ExternalInput").ap()
    dq = nc.dram_tensor("qv", [H, DH], F32, kind="ExternalInput").ap()
    dk = nc.dram_tensor("kv", [H, DH], F32, kind="ExternalInput").ap()
    dld = nc.dram_tensor("logdec", [H, 1], F32, kind="ExternalInput").ap()
    dgam = nc.dram_tensor("gam", [128, NF], F32, kind="ExternalInput").ap()
    dowT = nc.dram_tensor("owT", [128, NF, D], F32, kind="ExternalInput").ap()
    dA1T = nc.dram_tensor("a1T", [128, NF, 512], F32, kind="ExternalInput").ap()
    dW1B = nc.dram_tensor("w1B", [NE * RANK, FF], F32, kind="ExternalInput").ap()
    dW2AT = nc.dram_tensor("w2AT", [NPAIR, 128, 32, 128], F32, kind="ExternalInput").ap()
    dW2B = nc.dram_tensor("w2B", [128, NPAIR, D], F32, kind="ExternalInput").ap()
    didx = nc.dram_tensor("idx", [128, 8, 2], I32, kind="ExternalInput").ap()
    dwgt = nc.dram_tensor("wgt", [128, 8, 2], F32, kind="ExternalInput").ap()
    dIf = nc.dram_tensor("If32", [128, 128], F32, kind="ExternalInput").ap()
    dIb = nc.dram_tensor("Ibf", [128, 128], BF16, kind="ExternalInput").ap()
    dSEL = nc.dram_tensor("SELf", [H, 128], BF16, kind="ExternalInput").ap()
    dM16 = nc.dram_tensor("m16x8", [H, NF], F32, kind="ExternalInput").ap()
    dPS = nc.dram_tensor("psel", [NE, NPAIR * 128], BF16, kind="ExternalInput").ap()
    dout = nc.dram_tensor("out", [4, CH, D], F32, kind="ExternalOutput").ap()
    ddbg = {}
    for nm, shape in dict(snap=[128, 5, 8], cm=[128, 8], cv=[128, 8],
                          cS=[128, 8], cs2=[128, 8], cb2=[128, 8],
                          l16=[H, 1], r3=[H, 3 * NF], selw=[H, 128],
                          u=[128, 2, NF, 512], r=[1, CH], rc=[128, 8],
                          h1=[128, 2, D], a=[128, NPAIR, 512],
                          m=[NE, 512], M=[128, NPAIR, 512],
                          cs=[128, NPAIR, 512], h2=[128, 8, D],
                          lam=[128, NF], gc=[128, NF], c2=[128, NF]).items():
        if nm in debug_names:
            ddbg[nm] = nc.dram_tensor("dbg_" + nm, shape,
                                      F32 if nm in ("r", "rc", "h1", "lam", "gc", "c2", "l16", "r3", "cm", "cv", "cS", "cs2", "cb2", "snap") else BF16,
                                      kind="ExternalOutput").ap()

    es = ExitStack()
    with es:
        E = es.enter_context
        o_wTb = E(nc.sbuf_tensor("o_wTb", [128, NF, D], BF16))
        A1b = E(nc.sbuf_tensor("A1b", [128, NF, 512], BF16))
        W2Bb = E(nc.sbuf_tensor("W2Bb", [128, NPAIR, D], BF16))
        W1Bs = E(nc.sbuf_tensor("W1Bs", [128, 2, FF], BF16))
        W2As = E(nc.sbuf_tensor("W2As", [128, 2, 32, 128], BF16))
        xfm = E(nc.sbuf_tensor("xfm", [128, 2, NF, 512], BF16))
        xb = E(nc.sbuf_tensor("xb", [128, 2, 2, 2 * D], BF16))
        ub = E(nc.sbuf_tensor("ub", [128, 2, NF, 512], BF16))
        x32 = E(nc.sbuf_tensor("x32", [128, 2, D], F32))
        sqb = E(nc.sbuf_tensor("sqb", [128, 2, NF, 512], BF16))
        h1 = E(nc.sbuf_tensor("h1", [128, 2, D], F32))
        trash = E(nc.sbuf_tensor("trash", [128, 512], BF16))
        trash2 = E(nc.sbuf_tensor("trash2", [128, 8], F32))
        outb = E(nc.sbuf_tensor("outb", [128, 2, D], F32))
        ln1o = E(nc.sbuf_tensor("ln1o", [128, 8, D], BF16))
        h2 = E(nc.sbuf_tensor("h2", [128, 8, D], BF16))
        carryT = E(nc.sbuf_tensor("carryT", [128, 2, NF], F32))
        rcolT = E(nc.sbuf_tensor("rcolT", [128, 8], F32))
        asb = E(nc.sbuf_tensor("asb", [128, NPAIR, 512], BF16))
        csb = E(nc.sbuf_tensor("csb", [128, NPAIR, 512], BF16))
        htile = E(nc.sbuf_tensor("htile", [128, 3, 1024], BF16))
        Msb = E(nc.sbuf_tensor("Msb", [128, NPAIR, 512], BF16))
        mrows = E(nc.sbuf_tensor("mrows", [NE, 512], BF16))
        mcols = E(nc.sbuf_tensor("mcols", [128, 4, NE], F32))
        idxf = E(nc.sbuf_tensor("idxf", [128, 8, 2], F32))
        wgtf = E(nc.sbuf_tensor("wgtf", [128, 8, 2], F32))
        qh = E(nc.sbuf_tensor("qh", [H, DH], F32))
        kh = E(nc.sbuf_tensor("kh", [H, DH], F32))
        ldc = E(nc.sbuf_tensor("ldc", [H, 1], F32))
        lam16 = E(nc.sbuf_tensor("lam16", [H, 1], F32))
        c16 = E(nc.sbuf_tensor("c16", [H, 1], F32))
        m16 = E(nc.sbuf_tensor("m16", [H, NF], F32))
        rhs3 = E(nc.sbuf_tensor("rhs3", [H, 3 * NF], F32))
        rhs3h = E(nc.sbuf_tensor("rhs3h", [H, 3 * NF], BF16))
        rhs3l = E(nc.sbuf_tensor("rhs3l", [H, 3 * NF], BF16))
        mch = E(nc.sbuf_tensor("mch", [128, 4, NE], BF16))
        mcl = E(nc.sbuf_tensor("mcl", [128, 4, NE], BF16))
        lamcol = E(nc.sbuf_tensor("lamcol", [128, NF], F32))
        gccol = E(nc.sbuf_tensor("gccol", [128, NF], F32))
        c2col = E(nc.sbuf_tensor("c2col", [128, NF], F32))
        gamcol = E(nc.sbuf_tensor("gamcol", [128, NF], F32))
        SELf = E(nc.sbuf_tensor("sSELf", [H, 128], BF16))
        Ibf = E(nc.sbuf_tensor("sIbf", [128, 128], BF16))
        psel = E(nc.sbuf_tensor("spsel", [NE, NPAIR * 128], BF16))
        ones1f = E(nc.sbuf_tensor("ones1f", [1, 1], F32))
        cst = E(nc.sbuf_tensor("cst", [128, 4], F32))
        onescb = E(nc.sbuf_tensor("onescb", [128, 1], BF16))
        colm = E(nc.sbuf_tensor("colm", [128, 8], F32))
        colv = E(nc.sbuf_tensor("colv", [128, 8], F32))
        cols_ = E(nc.sbuf_tensor("cols_", [128, 8], F32))
        colw = E(nc.sbuf_tensor("colw", [128, 8], F32))
        colt = E(nc.sbuf_tensor("colt", [128, 8], F32))
        cols2 = E(nc.sbuf_tensor("cols2", [128, 8], F32))
        colb2 = E(nc.sbuf_tensor("colb2", [128, 8], F32))
        colhs = E(nc.sbuf_tensor("colhs", [128, 8, 2], F32))
        colsq = E(nc.sbuf_tensor("colsq", [128, 8], F32))
        snap = E(nc.sbuf_tensor("snap", [128, 5, 8], F32))
        p3m = E(nc.sbuf_tensor("p3m", [128, 8, 2], F32))
        p3q = E(nc.sbuf_tensor("p3q", [128, 8, 2], F32))

        psA = E(nc.psum_tensor("psA", [128, D], F32))
        psB = E(nc.psum_tensor("psB", [128, D], F32))
        psR = E(nc.psum_tensor("psR", [128, D], F32))
        psY = E(nc.psum_tensor("psY", [128, D], F32))
        bview = [psA[:, 0:512], psA[:, 512:D], psB[:, 0:512], psB[:, 512:D]]

        sem_names = [
            "wgtA", "wgtB", "x32d0", "x32d1", "xbd0", "xbd1",
            "trp0", "trp1", "outd0", "outd1", "p2d0", "p2d1", "idxd",
            "s0", "prep", "scn", "sqv", "smm", "rln", "vself", "sself",
            "owt", "h1v", "vst", "ssc", "vw", "stc", "vsb", "nrm",
            "mcv", "mtm", "mev", "mp", "mpe", "amm", "aev",
            "bmm", "gel", "cmf", "cev", "ymm", "h2v",
            "v3a", "s3a", "v3b", "nrm3", "dbgs",
        ]
        sems = {n: E(nc.semaphore(n)) for n in sem_names}
        CNT = {n: 0 for n in sem_names}
        SCHED = {k: [] for k in ("sync", "gpsimd", "tensor", "vector", "scalar")}

        def emit(eng, fn):
            SCHED[eng].append(fn)

        def load_act_set(s, set_id):
            s.add_instruction(mybir.InstLoadActFuncSet(
                name=nc.get_next_instruction_name(),
                act_func_set_id=set_id, ins=[], outs=[]))
            s.drain()

        def inc(name, n=1):
            CNT[name] += n
            return CNT[name]

        VSELF = [0]
        SSELF = [0]

        def vsync(v, inst):
            VSELF[0] += 1
            inst.then_inc(sems["vself"], 1)
            v.wait_ge(sems["vself"], VSELF[0])

        def ssync(s, inst):
            SSELF[0] += 1
            inst.then_inc(sems["sself"], 1)
            s.wait_ge(sems["sself"], SSELF[0])

        def g_loads(g):
            g.dma_start(ldc[:], dld).then_inc(sems["wgtA"], 16)
            g.dma_start(Ibf[:], dIb).then_inc(sems["wgtA"], 16)
            g.dma_start(SELf[:], dSEL).then_inc(sems["wgtA"], 16)
            g.dma_start(m16[:], dM16).then_inc(sems["wgtA"], 16)
            g.dma_start(psel[:], dPS).then_inc(sems["wgtA"], 16)
            g.dma_start(qh[:], dq).then_inc(sems["wgtA"], 16)
            g.dma_start(kh[:], dk).then_inc(sems["wgtA"], 16)
            g.dma_start(gamcol[:], dgam).then_inc(sems["wgtA"], 16)
            g.dma_start(o_wTb[:], dowT).then_inc(sems["wgtB"], 16)
            g.dma_start(A1b[:], dA1T).then_inc(sems["wgtB"], 16)
            g.dma_start(W2Bb[:], dW2B).then_inc(sems["wgtB"], 16)
            g.dma_start(idxf[:], didx).then_inc(sems["idxd"], 16)
            g.dma_start(wgtf[:], dwgt).then_inc(sems["idxd"], 16)
            for gp in range(2):
                g.dma_start(W1Bs[:, gp % 2, :], dW1B[128 * gp:128 * (gp + 1), :]
                            ).then_inc(sems["p2d%d" % (gp % 2)], 16)
                g.dma_start(W2As[:, gp % 2, :, :],
                            dW2AT[gp]).then_inc(sems["p2d%d" % (gp % 2)], 16)
        emit("gpsimd", g_loads)
        WGTA_ALL = inc("wgtA", 16 * 8)
        WGTB_ALL = inc("wgtB", 16 * 3)
        inc("idxd", 32)
        P2D_AFTER = {0: inc("p2d0", 32), 1: inc("p2d1", 32)}

        def s_prep(s):
            s.wait_ge(sems["wgtA"], WGTA_ALL)
            load_act_set(s, 2)
            s.activation(lam16[:], ldc[:], AF.Sigmoid).then_inc(sems["s0"], 1)
            load_act_set(s, 6)
        emit("scalar", s_prep)
        inc("s0", 1)

        def v_prep(v):
            v.memset(ones1f[:], 1.0)
            v.memset(onescb[:], 1.0)
            v.memset(cst[:, 0:1], RMS_EPS)
            v.memset(cst[:, 1:2], LN_EPS)
            v.memset(cst[:, 2:3], 1.0 / D)
            v.memset(cst[:, 3:4], -0.5)
            v.wait_ge(sems["wgtA"], WGTA_ALL)
            v.wait_ge(sems["s0"], 1)
            _i = v.tensor_mul(trash[:].bitcast(F32)[0:H, 0:DH], qh[:], kh[:])
            vsync(v, _i)
            v.reduce_sum(c16[:], trash[:].bitcast(F32)[0:H, 0:DH],
                         axis=mybir.AxisListType.X)
            v.tensor_scalar(rhs3[:, 0:NF], m16[:], lam16[:, 0:1], None, OP.mult)
            _i = v.tensor_scalar(rhs3[:, NF:2 * NF], m16[:], c16[:, 0:1], None, OP.mult)
            vsync(v, _i)
            _i = v.tensor_scalar(rhs3[:, 2 * NF:3 * NF], rhs3[:, NF:2 * NF],
                                 c16[:, 0:1], None, OP.mult)
            vsync(v, _i)
            _i = v.tensor_copy(rhs3h[:], rhs3[:])
            vsync(v, _i)
            v.tensor_sub(rhs3l[:], rhs3[:], rhs3h[:]).then_inc(sems["prep"], 1)
        emit("vector", v_prep)
        inc("prep", 1)

        def t_prep(t):
            t.wait_ge(sems["prep"], 1)
            t.matmul(psR[:, 0:3 * NF], SELf[:], rhs3h[:], start=True, stop=False)
            t.matmul(psR[:, 0:3 * NF], SELf[:], rhs3l[:], start=False, stop=True
                     ).then_inc(sems["mtm"], 1)
        emit("tensor", t_prep)
        inc("mtm", 1)

        def v_prep2(v):
            v.wait_ge(sems["mtm"], 1)
            v.wait_ge(sems["wgtB"], WGTB_ALL)
            v.tensor_copy(lamcol[:], psR[:, 0:NF])
            v.tensor_copy(c2col[:], psR[:, 2 * NF:3 * NF])
            _i = v.tensor_mul(gccol[:], gamcol[:], psR[:, NF:2 * NF])
            vsync(v, _i)
            for j in range(NF):
                for hf in range(2):
                    _i = v.tensor_copy(trash[:, :],
                                       o_wTb[:, j, hf * 512:(hf + 1) * 512])
                    vsync(v, _i)
                    last = v.tensor_scalar(o_wTb[:, j, hf * 512:(hf + 1) * 512],
                                           trash[:, :], gccol[:, j:j + 1],
                                           None, OP.mult)
                    if j == NF - 1 and hf == 1:
                        last.then_inc(sems["prep"], 1)
                    else:
                        vsync(v, last)
        emit("vector", v_prep2)
        PREP_ALL = inc("prep", 1)

        TRP_AFTER = {}
        SCN_AFTER = {}
        OWT_AFTER = {}
        SMM_AFTER = {}
        RLN_AFTER = {}
        H1V_G = {}
        H1V_AT = {}
        NRM_AT = {}
        OUTD_PAR_AT = {}
        STORES_UPTO = {}
        XBD_AFTER = {}

        real_groups = [i for i, g in enumerate(SCAN_GROUPS) if g[0] == "real"]
        gt_counter = [0]
        stores = [0]

        def emit_sec_A(gi):
            kind, pos, blk, init = SCAN_GROUPS[gi]
            par = gi % 2
            if kind == "real":
                xsrc = dx[pos]
            else:
                xsrc = dxc2 if kind == "pfx2" else dxc3
            xbs = "xbd%d" % par
            trs = "trp%d" % par
            for hlf in range(2):
                tok0 = blk * 512 + hlf * 256
                trp_need = TRP_AFTER.get(gi - 2, 0)

                def g_xb(g, xsrc=xsrc, tok0=tok0, par=par, hlf=hlf,
                         trp_need=trp_need, trs=trs, xbs=xbs):
                    if trp_need > 0:
                        g.wait_ge(sems[trs], trp_need)
                    g.dma_start(xb[:, par, hlf, :].rearrange("p (i f) -> p i f", i=2),
                                xsrc[tok0:tok0 + 256, :]
                                .rearrange("(i p) f -> p i f", p=128)
                                ).then_inc(sems[xbs], 16)
                emit("gpsimd", g_xb)
                inc(xbs, 16)
            XBD_AFTER[gi] = CNT[xbs]
            scn_need = SCN_AFTER.get(gi - 2, 0)
            for tt in range(4):
                hlf, tloc = tt // 2, tt % 2

                def s_tr(s, par=par, hlf=hlf, tloc=tloc, tt=tt,
                         xbd_n=XBD_AFTER[gi], scn_need=scn_need, trs=trs, xbs=xbs,
                         o0=CNT["outd0"], o1=CNT["outd1"]):
                    if scn_need > 0:
                        s.wait_ge(sems["scn"], scn_need)
                    s.wait_ge(sems[xbs], xbd_n)
                    if o0 > 0:
                        s.wait_ge(sems["outd0"], o0)
                    if o1 > 0:
                        s.wait_ge(sems["outd1"], o1)
                    s.dma_start(xfm[:, par, :, tt * 128:(tt + 1) * 128],
                                xb[:, par, hlf, tloc * D:(tloc + 1) * D]
                                .rearrange("p (j f) -> p j f", j=NF),
                                transpose=True).then_inc(sems[trs], 16)
                emit("scalar", s_tr)
                inc(trs, 16)
            TRP_AFTER[gi] = CNT[trs]
            owt_need = OWT_AFTER.get(gi - 2, 0)
            smm_need = SMM_AFTER.get(gi - 2, 0)

            def v_scan(v, par=par, init=init, trp_n=TRP_AFTER[gi],
                       owt_need=owt_need, smm_need=smm_need, trs=trs,
                       scn_prev=CNT["scn"], scn_base8=CNT["scn"] + 8):
                if init == "carry" and scn_prev > 0:
                    v.wait_ge(sems["scn"], scn_prev)
                if owt_need > 0:
                    v.wait_ge(sems["owt"], owt_need)
                if smm_need > 0:
                    v.wait_ge(sems["smm"], smm_need)
                v.wait_ge(sems[trs], trp_n)
                v.wait_ge(sems["prep"], PREP_ALL)
                for j in range(NF):
                    init_ap = 0.0 if init == "zero" else carryT[:, 1 - par, j:j + 1]
                    v.tensor_tensor_scan(ub[:, par, j, :],
                                         lamcol[:, j:j + 1].broadcast_to([128, 512]),
                                         xfm[:, par, j, :], init_ap,
                                         OP.mult, OP.add).then_inc(sems["scn"], 1)
                v.wait_ge(sems["scn"], scn_base8)
                v.tensor_copy(carryT[:, par, :], ub[:, par, :, 511:512]
                              ).then_inc(sems["scn"], 1)
            emit("vector", v_scan)
            inc("scn", 9)
            SCN_AFTER[gi] = CNT["scn"]

        def emit_sec_B(gi):
            sub = int(round((phases % 1) * 10)) if phases < 1 else 9
            _, pos, blk, _ = SCAN_GROUPS[gi]
            par = gi % 2
            rgidx = real_groups.index(gi)
            scn_n = SCN_AFTER[gi]
            gpar = rgidx % 2
            smm_prev = SMM_AFTER.get(real_groups[rgidx - 2], 0) if rgidx >= 2 else 0
            for j in range(NF):
                def v_sq(v, par=par, j=j, gpar=gpar, scn_n=scn_n,
                         smm_need=smm_prev if j == 0 else 0):
                    v.wait_ge(sems["scn"], scn_n)
                    if smm_need > 0:
                        v.wait_ge(sems["smm"], smm_need)
                    v.scalar_tensor_tensor(sqb[:, gpar, j, :], ub[:, par, j, :],
                                           c2col[:, j:j + 1], ub[:, par, j, :],
                                           OP.mult, OP.mult).then_inc(sems["sqv"], 1)
                emit("vector", v_sq)
                inc("sqv", 1)
            sqv_all = CNT["sqv"]
            if sub < 4:
                SMM_AFTER[gi] = CNT["smm"]
                return
            rln_need = RLN_AFTER.get(real_groups[rgidx - 1], 0) if rgidx > 0 else 0
            first_real = (rgidx == 0)

            def t_ssq(t, blk=blk, gpar=gpar, sqv_n=sqv_all, rln_need=rln_need,
                      first=first_real):
                t.wait_ge(sems["sqv"], sqv_n)
                if rln_need > 0:
                    t.wait_ge(sems["rln"], rln_need)
                if first:
                    t.wait_ge(sems["prep"], PREP_ALL)
                for tt in range(4):
                    ti = blk * 4 + tt
                    last = None
                    for j in range(NF):
                        last = t.matmul(psR[:, 504 + ti:505 + ti],
                                        sqb[:, gpar, j, tt * 128:(tt + 1) * 128],
                                        onescb[:],
                                        start=(j == 0), stop=(j == NF - 1),
                                        skip_group_check=True)
                    last.then_inc(sems["smm"], 1)
            emit("tensor", t_ssq)
            inc("smm", 4)
            SMM_AFTER[gi] = CNT["smm"]
            if sub < 5:
                return
            h1v_g_need = H1V_G.get(real_groups[rgidx - 2], 0) if rgidx >= 2 else 0

            def s_r(s, blk=blk, smm_n=SMM_AFTER[gi], h1v_g_need=h1v_g_need):
                s.wait_ge(sems["smm"], smm_n)
                if h1v_g_need > 0:
                    s.wait_ge(sems["h1v"], h1v_g_need)
                sl = slice(blk * 4, blk * 4 + 4)
                psl = slice(504 + blk * 4, 508 + blk * 4)
                _i = s.activation(rcolT[:, sl], psR[:, psl],
                                  AF.Ln, bias=cst[:, 0:1], scale=cst[:, 2:3])
                ssync(s, _i)
                s.activation(rcolT[:, sl], rcolT[:, sl],
                             AF.Exp, scale=cst[:, 3:4]).then_inc(sems["rln"], 1)
            emit("scalar", s_r)
            inc("rln", 1)
            RLN_AFTER[gi] = CNT["rln"]

        def emit_sec_C(gi):
            sub = int(round((phases % 1) * 10)) if phases < 1 else 10
            _, pos, blk, _ = SCAN_GROUPS[gi]
            par = gi % 2
            is_ffn = (pos == 3)
            for tt in range(4):
                ti = blk * 4 + tt
                gt = gt_counter[0]
                gt_counter[0] += 1
                tpar = gt % 2
                ps = psA if tpar == 0 else psB
                h1v_need = H1V_AT.get(gt - 2, 0)

                x3s = "x32d%d" % tpar

                def g_x32(g, pos=pos, ti=ti, tpar=tpar, h1v_need=h1v_need,
                          x3s=x3s):
                    if h1v_need > 0:
                        g.wait_ge(sems["h1v"], h1v_need)
                    g.dma_start(x32[:, tpar, :],
                                dx[pos, ti * 128:(ti + 1) * 128, :]
                                ).then_inc(sems[x3s], 16)
                emit("gpsimd", g_x32)
                inc(x3s, 16)
                x32_n = CNT[x3s]

                def t_ow(t, par=par, ti=ti, ps=ps, h1v_need=h1v_need):
                    if h1v_need > 0:
                        t.wait_ge(sems["h1v"], h1v_need)
                    tloc = (ti % 4) * 128
                    last = None
                    for j in range(NF):
                        for nh in range(2):
                            last = t.matmul(ps[:, nh * 512:(nh + 1) * 512],
                                            ub[:, par, j, tloc:tloc + 128],
                                            o_wTb[:, j, nh * 512:(nh + 1) * 512],
                                            start=(j == 0), stop=(j == NF - 1))
                    last.then_inc(sems["owt"], 1)
                emit("tensor", t_ow)
                inc("owt", 1)
                owt_n = CNT["owt"]
                nrm_need = NRM_AT.get(gt - 2, 0)

                def v_h1(v, ti=ti, tpar=tpar, ps=ps, owt_n=owt_n,
                         x32_n=x32_n, rln_n=CNT["rln"], nrm_need=nrm_need,
                         x3s=x3s):
                    v.wait_ge(sems["owt"], owt_n)
                    v.wait_ge(sems[x3s], x32_n)
                    v.wait_ge(sems["rln"], rln_n)
                    if nrm_need > 0 and sub >= 10:
                        v.wait_ge(sems["nrm"], nrm_need)
                    for nh in range(2):
                        _i = v.scalar_tensor_tensor(h1[:, tpar, nh * 512:(nh + 1) * 512],
                                                    ps[:, nh * 512:(nh + 1) * 512],
                                                    rcolT[:, ti:ti + 1],
                                                    x32[:, tpar, nh * 512:(nh + 1) * 512],
                                                    OP.mult, OP.add,
                                                    accum_out=colhs[:, ti, nh:nh + 1])
                        vsync(v, _i)
                    _i = v.scalar_tensor_tensor(trash[:, :], h1[:, tpar, 0:512], 1.0,
                                                h1[:, tpar, 0:512], OP.bypass, OP.mult,
                                                accum_out=colsq[:, ti:ti + 1])
                    vsync(v, _i)
                    _i = v.scalar_tensor_tensor(trash[:, :], h1[:, tpar, 512:D], 1.0,
                                                h1[:, tpar, 512:D], OP.bypass, OP.mult,
                                                accum_out=trash2[:, 1:2])
                    vsync(v, _i)
                    _i = v.tensor_add(colsq[:, ti:ti + 1], colsq[:, ti:ti + 1],
                                      trash2[:, 1:2])
                    _i2 = v.tensor_add(colm[:, ti:ti + 1], colhs[:, ti, 0:1],
                                       colhs[:, ti, 1:2])
                    vsync(v, _i2)
                    _i = v.tensor_scalar(colm[:, ti:ti + 1], colm[:, ti:ti + 1],
                                         1.0 / D, None, OP.mult)
                    vsync(v, _i)
                    _i = v.tensor_mul(trash2[:, 0:1], colm[:, ti:ti + 1],
                                      colm[:, ti:ti + 1])
                    vsync(v, _i)
                    v.scalar_tensor_tensor(colv[:, ti:ti + 1], colsq[:, ti:ti + 1],
                                           1.0 / D, trash2[:, 0:1],
                                           OP.mult, OP.subtract).then_inc(sems["h1v"], 1)
                emit("vector", v_h1)
                inc("h1v", 1)
                inc("vst", 1)
                H1V_AT[gt] = CNT["h1v"]
                vst_n = CNT["vst"]
                if sub < 9:
                    OUTD_PAR_AT[(tpar, gt)] = CNT["outd%d" % tpar]
                    for _pp in range(2):
                        if (_pp, gt) not in OUTD_PAR_AT:
                            OUTD_PAR_AT[(_pp, gt)] = OUTD_PAR_AT.get((_pp, gt - 1), 0)
                    STORES_UPTO[gt] = stores[0]
                    NRM_AT[gt] = CNT["nrm"]
                    continue

                def s_scol(s, ti=ti, vst_n=vst_n):
                    s.wait_ge(sems["h1v"], vst_n)
                    _i = s.activation(cols_[:, ti:ti + 1], colv[:, ti:ti + 1],
                                      AF.Ln, bias=cst[:, 1:2])
                    ssync(s, _i)
                    s.activation(cols_[:, ti:ti + 1], cols_[:, ti:ti + 1],
                                 AF.Exp, scale=cst[:, 3:4]).then_inc(sems["ssc"], 1)
                emit("scalar", s_scol)
                inc("ssc", 1)
                ssc_n = CNT["ssc"]

                if not is_ffn:
                    def v_w(v, ti=ti, ssc_n=ssc_n):
                        v.wait_ge(sems["ssc"], ssc_n)
                        _i = v.tensor_mul(colw[:, ti:ti + 1], cols_[:, ti:ti + 1],
                                          cols_[:, ti:ti + 1])
                        vsync(v, _i)
                        v.tensor_mul(colw[:, ti:ti + 1], colw[:, ti:ti + 1],
                                     colv[:, ti:ti + 1]).then_inc(sems["vw"], 1)
                    emit("vector", v_w)
                    inc("vw", 1)
                    vw_n = CNT["vw"]

                    def s_tcol(s, ti=ti, vw_n=vw_n):
                        s.wait_ge(sems["vw"], vw_n)
                        _i = s.activation(colt[:, ti:ti + 1], colw[:, ti:ti + 1],
                                          AF.Ln, bias=cst[:, 1:2])
                        ssync(s, _i)
                        s.activation(colt[:, ti:ti + 1], colt[:, ti:ti + 1],
                                     AF.Exp, scale=cst[:, 3:4]).then_inc(sems["stc"], 1)
                    emit("scalar", s_tcol)
                    inc("stc", 1)
                    stc_n = CNT["stc"]

                    def v_sb(v, ti=ti, stc_n=stc_n):
                        v.wait_ge(sems["stc"], stc_n)
                        _i = v.tensor_mul(cols2[:, ti:ti + 1], cols_[:, ti:ti + 1],
                                          colt[:, ti:ti + 1])
                        vsync(v, _i)
                        _i = v.tensor_mul(colb2[:, ti:ti + 1], colm[:, ti:ti + 1],
                                          cols2[:, ti:ti + 1])
                        vsync(v, _i)
                        v.tensor_scalar(colb2[:, ti:ti + 1], colb2[:, ti:ti + 1],
                                        -1.0, None, OP.mult).then_inc(sems["vsb"], 1)
                    emit("vector", v_sb)
                    inc("vsb", 1)
                else:
                    def v_sb(v, ti=ti, ssc_n=ssc_n):
                        v.wait_ge(sems["ssc"], ssc_n)
                        v.tensor_copy(cols2[:, ti:ti + 1], cols_[:, ti:ti + 1])
                        _i = v.tensor_mul(colb2[:, ti:ti + 1], colm[:, ti:ti + 1],
                                          cols_[:, ti:ti + 1])
                        vsync(v, _i)
                        v.tensor_scalar(colb2[:, ti:ti + 1], colb2[:, ti:ti + 1],
                                        -1.0, None, OP.mult).then_inc(sems["vsb"], 1)
                    emit("vector", v_sb)
                    inc("vsb", 1)
                vsb_n = CNT["vsb"]
                if sub < 10:
                    OUTD_PAR_AT[(tpar, gt)] = CNT["outd%d" % tpar]
                    for _pp in range(2):
                        if (_pp, gt) not in OUTD_PAR_AT:
                            OUTD_PAR_AT[(_pp, gt)] = OUTD_PAR_AT.get((_pp, gt - 1), 0)
                    STORES_UPTO[gt] = stores[0]
                    NRM_AT[gt] = CNT["nrm"]
                    continue
                ods = "outd%d" % tpar
                outd_need = OUTD_PAR_AT.get((tpar, gt - 2), 0)

                def s_nrm(s, ti=ti, tpar=tpar, is_ffn=is_ffn, vsb_n=vsb_n,
                          outd_need=outd_need if "nostore" not in debug_names else 0,
                          ods=ods):
                    s.wait_ge(sems["vsb"], vsb_n)
                    if is_ffn:
                        s.activation(ln1o[:, ti, :], h1[:, tpar, :], AF.Identity,
                                     bias=colb2[:, ti:ti + 1],
                                     scale=cols2[:, ti:ti + 1]).then_inc(sems["nrm"], 1)
                    else:
                        if outd_need > 0:
                            s.wait_ge(sems[ods], outd_need)
                        s.activation(outb[:, tpar, :], h1[:, tpar, :], AF.Identity,
                                     bias=colb2[:, ti:ti + 1],
                                     scale=cols2[:, ti:ti + 1]).then_inc(sems["nrm"], 1)
                emit("scalar", s_nrm)
                inc("nrm", 1)
                NRM_AT[gt] = CNT["nrm"]

                if not is_ffn and "nostore" not in debug_names:
                    def sy_out(sy, pos=pos, ti=ti, tpar=tpar, nrm_n=CNT["nrm"],
                               ods=ods, t0=CNT["trp0"], t1=CNT["trp1"]):
                        sy.wait_ge(sems["nrm"], nrm_n)
                        if t0 > 0:
                            sy.wait_ge(sems["trp0"], t0)
                        if t1 > 0:
                            sy.wait_ge(sems["trp1"], t1)
                        sy.dma_start(dout[pos, ti * 128:(ti + 1) * 128, :],
                                     outb[:, tpar, :]).then_inc(sems[ods], 16)
                    emit("sync", sy_out)
                    stores[0] += 1
                    inc(ods, 16)
                OUTD_PAR_AT[(tpar, gt)] = CNT[ods]
                for _pp in range(2):
                    if (_pp, gt) not in OUTD_PAR_AT:
                        OUTD_PAR_AT[(_pp, gt)] = OUTD_PAR_AT.get((_pp, gt - 1), CNT["outd%d" % _pp])
                STORES_UPTO[gt] = stores[0]
            OWT_AFTER[gi] = CNT["owt"]
            H1V_G[gi] = CNT["h1v"]

        def emit_snap(nrm_n):
            def v_snap(v, nrm_n=nrm_n):
                v.memset(snap[:], 0.0)
                v.wait_ge(sems["nrm"], nrm_n)
                v.tensor_copy(snap[:, 0, :], colm[:])
                v.tensor_copy(snap[:, 1, :], colv[:])
                v.tensor_copy(snap[:, 2, :], cols_[:])
                v.tensor_copy(snap[:, 3, :], cols2[:])
                v.tensor_copy(snap[:, 4, :], colb2[:]).then_inc(sems["dbgs"], 1)
            emit("vector", v_snap)

        pending_real = None
        for gi in range(len(SCAN_GROUPS)):
            emit_sec_A(gi)
            if SCAN_GROUPS[gi][0] == "real" and phases >= 0.3:
                emit_sec_B(gi)
            if pending_real is not None:
                emit_sec_C(pending_real)
                pending_real = None
            if SCAN_GROUPS[gi][0] == "real" and phases >= 0.75:
                pending_real = gi
            else:
                OWT_AFTER[gi] = CNT["owt"]
                if gi - 1 in OWT_AFTER:
                    pass
        if pending_real is not None:
            emit_sec_C(pending_real)
        if debug_names:
            emit_snap(NRM_AT.get(15, CNT["nrm"]))

        def _phase1_only():
            return phases < 2

        if phases >= 2:
            def s_gel_tab(s, nrm_n=CNT["nrm"]):
                s.wait_ge(sems["nrm"], nrm_n)
                load_act_set(s, 10)
            emit("scalar", s_gel_tab)

            GEL_AFTER = {}
            CMF_AFTER = {}
            H2V_AT = {}
            ff_g = [0]
            yunit = [0]
            CEV_AFTER = {}
            MPE_AFTER = {}

            for b2 in range(2):
                def v_m(v, b2=b2):
                    v.wait_ge(sems["idxd"], 32)
                    for ts in range(4):
                        tix = b2 * 4 + ts
                        for e in range(NE):
                            _i = v.scalar_tensor_tensor(trash2[:, 0:2], idxf[:, tix, :],
                                                        float(e), wgtf[:, tix, :],
                                                        OP.is_equal, OP.mult,
                                                        accum_out=mcols[:, ts, e:e + 1])
                            vsync(v, _i)
                    _i = v.tensor_copy(mch[:], mcols[:])
                    vsync(v, _i)
                    v.tensor_sub(mcl[:], mcols[:], mch[:]).then_inc(sems["mcv"], 1)
                emit("vector", v_m)
                inc("mcv", 1)

                def t_mt(t, mcv_n=CNT["mcv"], rln_n=CNT["rln"],
                         cev_n=CEV_AFTER.get((b2 - 1, NPAIR - 1), 0)):
                    t.wait_ge(sems["mcv"], mcv_n)
                    t.wait_ge(sems["rln"], rln_n)
                    if cev_n > 0:
                        t.wait_ge(sems["cev"], cev_n)
                    last = None
                    for ts in range(4):
                        t.matmul(psR[0:NE, ts * 128:(ts + 1) * 128],
                                 mch[:, ts, :], Ibf[:], start=True, stop=False)
                        last = t.matmul(psR[0:NE, ts * 128:(ts + 1) * 128],
                                        mcl[:, ts, :], Ibf[:], start=False, stop=True)
                    last.then_inc(sems["mtm"], 1)
                emit("tensor", t_mt)
                inc("mtm", 1)

                def v_mev(v, mtm_n=CNT["mtm"]):
                    v.wait_ge(sems["mtm"], mtm_n)
                    v.tensor_copy(mrows[:, :], psR[0:NE, 0:512]).then_inc(sems["mev"], 1)
                emit("vector", v_mev)
                inc("mev", 1)

                mpe_before = CNT["mpe"]

                def t_mp(t, mev_n=CNT["mev"], mpe_before=mpe_before):
                    t.wait_ge(sems["mev"], mev_n)
                    for p in range(NPAIR):
                        if p >= 2:
                            t.wait_ge(sems["mpe"], mpe_before + p - 1)
                        t.matmul(psR[:, (p % 2) * 512:(p % 2 + 1) * 512],
                                 psel[:, p * 128:(p + 1) * 128], mrows[:, :],
                                 start=True, stop=True).then_inc(sems["mp"], 1)
                emit("tensor", t_mp)
                mp_base = CNT["mp"]
                inc("mp", NPAIR)

                def v_mpe(v, mp_base=mp_base):
                    for p in range(NPAIR):
                        v.wait_ge(sems["mp"], mp_base + p + 1)
                        v.tensor_copy(Msb[:, p, :],
                                      psR[:, (p % 2) * 512:(p % 2 + 1) * 512]).then_inc(sems["mpe"], 1)
                emit("vector", v_mpe)
                inc("mpe", NPAIR)
                MPE_AFTER[b2] = CNT["mpe"]

                for tt in range(4):
                    tix = b2 * 4 + tt

                    def s_xf(s, tix=tix, tt=tt, b2=b2, nrm_n=CNT["nrm"],
                             scn_n=CNT["scn"], o0=CNT["outd0"], o1=CNT["outd1"]):
                        s.wait_ge(sems["nrm"], nrm_n)
                        s.wait_ge(sems["scn"], scn_n)
                        if o0 > 0:
                            s.wait_ge(sems["outd0"], o0)
                        if o1 > 0:
                            s.wait_ge(sems["outd1"], o1)
                        s.dma_start(xfm[:, b2, :, tt * 128:(tt + 1) * 128],
                                    ln1o[:, tix, :].rearrange("p (j f) -> p j f", j=NF),
                                    transpose=True).then_inc(sems["trp%d" % b2], 16)
                    emit("scalar", s_xf)
                    inc("trp%d" % b2, 16)
                trp_n2 = CNT["trp%d" % b2]

                AEV_AFTER = {}
                for rs in range(NPAIR):
                    aev_need = AEV_AFTER.get(rs - 2, 0)

                    def t_a(t, rs=rs, b2=b2, trp_n=trp_n2, mpe_n=MPE_AFTER[b2],
                            aev_need=aev_need):
                        t.wait_ge(sems["trp%d" % b2], trp_n)
                        t.wait_ge(sems["mpe"], mpe_n)
                        if aev_need > 0:
                            t.wait_ge(sems["aev"], aev_need)
                        last = None
                        for j in range(NF):
                            last = t.matmul(psR[:, (rs % 2) * 512:(rs % 2 + 1) * 512],
                                            A1b[:, j, rs * 128:(rs + 1) * 128],
                                            xfm[:, b2, j, :],
                                            start=(j == 0), stop=(j == NF - 1))
                        last.then_inc(sems["amm"], 1)
                    emit("tensor", t_a)
                    inc("amm", 1)

                    def v_aev(v, rs=rs, amm_n=CNT["amm"]):
                        v.wait_ge(sems["amm"], amm_n)
                        v.tensor_copy(asb[:, rs, :],
                                      psR[:, (rs % 2) * 512:(rs % 2 + 1) * 512]).then_inc(sems["aev"], 1)
                    emit("vector", v_aev)
                    inc("aev", 1)
                    AEV_AFTER[rs] = CNT["aev"]
                aev_all = CNT["aev"]

                for p in range(NPAIR):
                    gp = b2 * NPAIR + p
                    cev_need = CEV_AFTER.get((b2, p - 2), 0)
                    for ffi in range(32):
                        fg = ff_g[0]
                        ff_g[0] += 1
                        bb0 = (2 * fg) % 4
                        bb1 = (2 * fg + 1) % 4
                        gel_need = GEL_AFTER.get(fg - 2, 0)

                        def t_b(t, p=p, gp=gp, ffi=ffi, bb0=bb0, bb1=bb1,
                                aev_n=aev_all, p2d_n=P2D_AFTER[gp],
                                gel_need=gel_need):
                            t.wait_ge(sems["aev"], aev_n)
                            t.wait_ge(sems["p2d%d" % (gp % 2)], p2d_n)
                            if gel_need > 0:
                                t.wait_ge(sems["gel"], gel_need)
                            t.matmul(bview[bb0],
                                     W1Bs[0:64, gp % 2, ffi * 128:(ffi + 1) * 128],
                                     asb[0:64, p, :], start=True, stop=True,
                                     tile_position=(0, 0))
                            t.matmul(bview[bb1],
                                     W1Bs[64:128, gp % 2, ffi * 128:(ffi + 1) * 128],
                                     asb[64:128, p, :], start=True, stop=True,
                                     tile_position=(64, 0)).then_inc(sems["bmm"], 1)
                        emit("tensor", t_b)
                        inc("bmm", 1)
                        bmm_n = CNT["bmm"]
                        cmf_need = CMF_AFTER.get(fg - 3, 0)

                        def s_g(s, ffi=ffi, fg=fg, bb0=bb0, bb1=bb1, bmm_n=bmm_n,
                                cmf_need=cmf_need):
                            s.wait_ge(sems["bmm"], bmm_n)
                            if cmf_need > 0:
                                s.wait_ge(sems["cmf"], cmf_need)
                            s.activation(htile[:, fg % 3, 0:512],
                                         bview[bb0], AF.Gelu)
                            s.activation(htile[:, fg % 3, 512:1024],
                                         bview[bb1], AF.Gelu).then_inc(sems["gel"], 1)
                        emit("scalar", s_g)
                        inc("gel", 1)
                        GEL_AFTER[fg] = CNT["gel"]
                        gel_n = CNT["gel"]

                        def t_c(t, p=p, gp=gp, ffi=ffi, fg=fg, gel_n=gel_n,
                                cev_need=cev_need if ffi == 0 else 0):
                            t.wait_ge(sems["gel"], gel_n)
                            if cev_need > 0:
                                t.wait_ge(sems["cev"], cev_need)
                            t.matmul(psR[0:64, (p % 2) * 512:(p % 2 + 1) * 512],
                                     W2As[:, gp % 2, ffi, 0:64],
                                     htile[:, fg % 3, 0:512],
                                     start=(ffi == 0), stop=(ffi == 31),
                                     tile_position=(0, 0), skip_group_check=True)
                            t.matmul(psR[64:128, (p % 2) * 512:(p % 2 + 1) * 512],
                                     W2As[:, gp % 2, ffi, 64:128],
                                     htile[:, fg % 3, 512:1024],
                                     start=(ffi == 0), stop=(ffi == 31),
                                     tile_position=(0, 64), skip_group_check=True
                                     ).then_inc(sems["cmf"], 1)
                        emit("tensor", t_c)
                        inc("cmf", 1)
                        CMF_AFTER[fg] = CNT["cmf"]

                    def v_cev(v, p=p, b2=b2, cmf_n=CNT["cmf"], mpe_n=MPE_AFTER[b2]):
                        v.wait_ge(sems["cmf"], cmf_n)
                        v.wait_ge(sems["mpe"], mpe_n)
                        v.tensor_mul(csb[:, p, :],
                                     psR[:, (p % 2) * 512:(p % 2 + 1) * 512],
                                     Msb[:, p, :]).then_inc(sems["cev"], 1)
                    emit("vector", v_cev)
                    inc("cev", 1)
                    CEV_AFTER[(b2, p)] = CNT["cev"]

                    if gp + 2 < 8:
                        gpn = gp + 2

                        def g_w2(g, gpn=gpn, cmf_n=CNT["cmf"]):
                            g.wait_ge(sems["cmf"], cmf_n)
                            pn = gpn % NPAIR
                            g.dma_start(W1Bs[:, gpn % 2, :],
                                        dW1B[128 * pn:128 * (pn + 1), :]
                                        ).then_inc(sems["p2d%d" % (gpn % 2)], 16)
                            g.dma_start(W2As[:, gpn % 2, :, :],
                                        dW2AT[pn]).then_inc(sems["p2d%d" % (gpn % 2)], 16)
                        emit("gpsimd", g_w2)
                        P2D_AFTER[gpn] = inc("p2d%d" % (gpn % 2), 32)

                cev_all = CEV_AFTER[(b2, NPAIR - 1)]
                for ts in range(4):
                    tix = b2 * 4 + ts
                    for jh in range(2):
                        u = yunit[0]
                        yunit[0] += 1
                        ybank = u % 2
                        h2v_need = H2V_AT.get(u - 2, 0)

                        def t_y(t, tix=tix, ts=ts, jh=jh, ybank=ybank,
                                cev_n=cev_all, h2v_need=h2v_need, wg=WGTB_ALL):
                            t.wait_ge(sems["cev"], cev_n)
                            t.wait_ge(sems["wgtB"], wg)
                            if h2v_need > 0:
                                t.wait_ge(sems["h2v"], h2v_need)
                            yv = psY[:, ybank * 512:(ybank + 1) * 512]
                            t.matmul(yv, Ibf[:],
                                     ln1o[:, tix, jh * 512:(jh + 1) * 512],
                                     start=True, stop=False)
                            last = None
                            for p_ in range(NPAIR):
                                last = t.matmul(yv,
                                                csb[:, p_, ts * 128:(ts + 1) * 128],
                                                W2Bb[:, p_, jh * 512:(jh + 1) * 512],
                                                start=False, stop=(p_ == NPAIR - 1))
                            last.then_inc(sems["ymm"], 1)
                        emit("tensor", t_y)
                        inc("ymm", 1)

                        def v_h2(v, tix=tix, jh=jh, ybank=ybank, ymm_n=CNT["ymm"],
                                 h2v_self=CNT["h2v"]):
                            v.wait_ge(sems["ymm"], ymm_n)
                            if h2v_self > 0:
                                v.wait_ge(sems["h2v"], h2v_self)
                            yv = psY[:, ybank * 512:(ybank + 1) * 512]
                            _i = v.tensor_scalar(h2[:, tix, jh * 512:(jh + 1) * 512],
                                                 yv, 1.0, 0.0, OP.mult, OP.add,
                                                 accum_out=p3m[:, tix, jh:jh + 1])
                            vsync(v, _i)
                            v.scalar_tensor_tensor(trash[:, 0:512],
                                                   h2[:, tix, jh * 512:(jh + 1) * 512],
                                                   1.0,
                                                   h2[:, tix, jh * 512:(jh + 1) * 512],
                                                   OP.bypass, OP.mult,
                                                   accum_out=p3q[:, tix, jh:jh + 1]).then_inc(sems["h2v"], 1)
                        emit("vector", v_h2)
                        inc("h2v", 1)
                        H2V_AT[u] = CNT["h2v"]

            def s_p3_tab(s, gel_n=CNT["gel"]):
                s.wait_ge(sems["gel"], gel_n)
                load_act_set(s, 6)
            emit("scalar", s_p3_tab)

            P3_OUTD_AT = {}
            P3_OUTD_AT[-2] = CNT["outd0"]
            P3_OUTD_AT[-1] = CNT["outd1"]
            for tix in range(8):
                h2v_need = H2V_AT[tix * 2 + 1]

                def v_3a(v, tix=tix, h2v_need=h2v_need):
                    v.wait_ge(sems["h2v"], h2v_need)
                    _i = v.tensor_add(colm[:, tix:tix + 1], p3m[:, tix, 0:1],
                                      p3m[:, tix, 1:2])
                    vsync(v, _i)
                    _i = v.tensor_scalar(colm[:, tix:tix + 1], colm[:, tix:tix + 1],
                                         1.0 / D, None, OP.mult)
                    v.tensor_add(colsq[:, tix:tix + 1], p3q[:, tix, 0:1], p3q[:, tix, 1:2])
                    vsync(v, _i)
                    _i = v.tensor_mul(trash2[:, 0:1], colm[:, tix:tix + 1],
                                      colm[:, tix:tix + 1])
                    vsync(v, _i)
                    v.scalar_tensor_tensor(colv[:, tix:tix + 1], colsq[:, tix:tix + 1],
                                           1.0 / D, trash2[:, 0:1], OP.mult, OP.subtract).then_inc(sems["v3a"], 1)
                emit("vector", v_3a)
                inc("v3a", 1)

                def s_3a(s, tix=tix, v3a_n=CNT["v3a"]):
                    s.wait_ge(sems["v3a"], v3a_n)
                    s.activation(cols_[:, tix:tix + 1], colv[:, tix:tix + 1],
                                 AF.Ln, bias=cst[:, 1:2])
                    s.activation(cols_[:, tix:tix + 1], cols_[:, tix:tix + 1],
                                 AF.Exp, scale=cst[:, 3:4]).then_inc(sems["s3a"], 1)
                emit("scalar", s_3a)
                inc("s3a", 1)

                def v_3b(v, tix=tix, s3a_n=CNT["s3a"]):
                    v.wait_ge(sems["s3a"], s3a_n)
                    _i = v.tensor_mul(colb2[:, tix:tix + 1], colm[:, tix:tix + 1],
                                      cols_[:, tix:tix + 1])
                    vsync(v, _i)
                    v.tensor_scalar(colb2[:, tix:tix + 1], colb2[:, tix:tix + 1],
                                    -1.0, None, OP.mult).then_inc(sems["v3b"], 1)
                emit("vector", v_3b)
                inc("v3b", 1)
                ods3 = "outd%d" % (tix % 2)
                outd_need3 = P3_OUTD_AT[tix - 2]

                def s_3n(s, tix=tix, v3b_n=CNT["v3b"], outd_need=outd_need3,
                         ods3=ods3):
                    s.wait_ge(sems["v3b"], v3b_n)
                    if outd_need > 0:
                        s.wait_ge(sems[ods3], outd_need)
                    s.activation(outb[:, tix % 2, :], h2[:, tix, :], AF.Identity,
                                 bias=colb2[:, tix:tix + 1], scale=cols_[:, tix:tix + 1]).then_inc(sems["nrm3"], 1)
                emit("scalar", s_3n)
                inc("nrm3", 1)

                def sy_3o(sy, tix=tix, nrm3_n=CNT["nrm3"], ods3=ods3):
                    sy.wait_ge(sems["nrm3"], nrm3_n)
                    sy.dma_start(dout[3, tix * 128:(tix + 1) * 128, :],
                                 outb[:, tix % 2, :]).then_inc(sems[ods3], 16)
                emit("sync", sy_3o)
                stores[0] += 1
                inc(ods3, 16)
                P3_OUTD_AT[tix] = CNT[ods3]

        if ddbg:
            def sy_dbg(sy, targets=dict(ddbg), o0=CNT["outd0"], o1=CNT["outd1"],
                       want_snap=("snap" in ddbg)):
                sy.wait_ge(sems["outd0"], o0)
                sy.wait_ge(sems["outd1"], o1)
                if want_snap:
                    sy.wait_ge(sems["dbgs"], 1)
                nd = 0
                for nm, ap in targets.items():
                    src = {"snap": snap, "cm": colm, "cv": colv, "cS": cols_, "cs2": cols2,
                           "cb2": colb2,
                           "l16": lam16, "r3": rhs3, "selw": SELf,
                           "u": ub, "rc": rcolT, "h1": h1, "a": asb,
                           "m": mrows, "M": Msb, "cs": csb, "h2": h2,
                           "lam": lamcol, "gc": gccol, "c2": c2col}[nm]
                    sy.dma_start(ap, src[:]).then_inc(sems["dbgs"], 16)
                    nd += 1
                sy.wait_ge(sems["dbgs"], 16 * nd + (1 if want_snap else 0))
            emit("sync", sy_dbg)

        def sy_fin(sy, o0=CNT["outd0"], o1=CNT["outd1"]):
            sy.wait_ge(sems["outd0"], o0)
            sy.wait_ge(sems["outd1"], o1)
        emit("sync", sy_fin)

        block = E(nc.Block())

        @block.gpsimd
        def _(g):
            for fn in SCHED["gpsimd"]:
                fn(g)

        @block.scalar
        def _(s):
            for fn in SCHED["scalar"]:
                fn(s)

        @block.tensor
        def _(t):
            for fn in SCHED["tensor"]:
                fn(t)

        @block.vector
        def _(v):
            for fn in SCHED["vector"]:
                fn(v)

        @block.sync
        def _(sy):
            for fn in SCHED["sync"]:
                fn(sy)

    return nc


def _core_chunks(c):
    return [2 * (4 + c), 2 * (4 + c) + 1,
            2 * (12 + c // 2) + ((c + 1) % 2),
            c]


def _consts():
    SELf = np.zeros((H, 128), dtype=np.float32)
    for h in range(H):
        for p in range(128):
            if h % 2 == p // 64:
                SELf[h, p] = 1.0
    m16 = np.zeros((H, NF), dtype=np.float32)
    for h in range(H):
        m16[h, h // 2] = 1.0
    psel = np.zeros((NE, NPAIR * 128), dtype=np.float32)
    for p in range(NPAIR):
        psel[2 * p, p * 128:p * 128 + 64] = 1.0
        psel[2 * p + 1, p * 128 + 64:p * 128 + 128] = 1.0
    return (np.eye(128, dtype=np.float32),
            np.eye(128, dtype=np.float32).astype(BF),
            SELf, m16, psel.astype(BF))


def kernel(L0_attn_x, L0_attn_q, L0_attn_k, L0_attn_log_decay,
           L0_attn_norm_weight, L0_attn_o_w, L0_ffn_expert_weights,
           L0_ffn_expert_indices, L0_ffn_w1_A, L0_ffn_w1_B, L0_ffn_w2_A,
           L0_ffn_w2_B):
    x = np.ascontiguousarray(np.asarray(L0_attn_x, dtype=np.float32)
                             ).reshape(32 * CH, D)
    owT = np.ascontiguousarray(np.asarray(L0_attn_o_w, dtype=np.float32).T
                               .reshape(NF, 128, D).transpose(1, 0, 2))
    a1T = np.ascontiguousarray(np.asarray(L0_ffn_w1_A, dtype=np.float32).T
                               .reshape(NF, 128, 512).transpose(1, 0, 2))
    w1B = np.ascontiguousarray(np.asarray(L0_ffn_w1_B, dtype=np.float32))
    _w2at = np.asarray(L0_ffn_w2_A, dtype=np.float32).T
    w2AT = np.ascontiguousarray(np.stack([
        _w2at[:, 128 * p:128 * (p + 1)].reshape(32, 128, 128).transpose(1, 0, 2)
        for p in range(NPAIR)]))
    w2B = np.ascontiguousarray(np.asarray(L0_ffn_w2_B, dtype=np.float32)
                               .reshape(NPAIR, 128, D).transpose(1, 0, 2))
    qv = np.ascontiguousarray(np.asarray(L0_attn_q, dtype=np.float32).reshape(H, DH))
    kv = np.ascontiguousarray(np.asarray(L0_attn_k, dtype=np.float32).reshape(H, DH))
    ld = np.ascontiguousarray(np.asarray(L0_attn_log_decay,
                                         dtype=np.float32).reshape(H, 1))
    gam = np.ascontiguousarray(np.asarray(L0_attn_norm_weight, dtype=np.float32).reshape(NF, 128).T)
    eidx = np.ascontiguousarray(np.asarray(L0_ffn_expert_indices, dtype=np.int32))
    ewgt = np.ascontiguousarray(np.asarray(L0_ffn_expert_weights, dtype=np.float32))
    If, Ib, SELf, m16, psel = _consts()
    zeros_ch = np.zeros((CH, D), dtype=np.float32)

    nc = build_program()
    in_maps = []
    for c in range(8):
        chunks = _core_chunks(c)
        xo = np.stack([x[k * CH:(k + 1) * CH, :] for k in chunks])
        def prefix(k):
            b, hlf = k // 2, k % 2
            if hlf == 1:
                return np.ascontiguousarray(x[b * 2 * CH:(b * 2 + 1) * CH, :])
            return zeros_ch
        in_maps.append({
            "x": xo, "xc2": prefix(chunks[2]), "xc3": prefix(chunks[3]),
            "qv": qv, "kv": kv, "logdec": ld, "gam": gam, "owT": owT,
            "a1T": a1T, "w1B": w1B, "w2AT": w2AT, "w2B": w2B,
            "idx": np.ascontiguousarray(eidx[c * CH:(c + 1) * CH, :]
                                        .reshape(8, 128, 2).transpose(1, 0, 2)),
            "wgt": np.ascontiguousarray(ewgt[c * CH:(c + 1) * CH, :]
                                        .reshape(8, 128, 2).transpose(1, 0, 2)),
            "If32": If, "Ibf": Ib, "SELf": SELf.astype(BF), "m16x8": m16, "psel": psel,
        })
    res = run_bass_kernel_spmd(nc, in_maps, core_ids=list(range(8)))
    out = np.empty((32 * CH, D), dtype=np.float32)
    for c in range(8):
        oc = res.results[c]["out"]
        for pos, k in enumerate(_core_chunks(c)):
            out[k * CH:(k + 1) * CH, :] = oc[pos]
    return out.reshape(16, 2048, D)


# version 02 (final)
# speedup vs baseline: 1.0102x; 1.0102x over previous
import numpy as np
import ml_dtypes
import concourse.bass as bass
import concourse.mybir as mybir
from concourse.bass_utils import run_bass_kernel_spmd
from contextlib import ExitStack

F32 = mybir.dt.float32
BF16 = mybir.dt.bfloat16
I32 = mybir.dt.int32
AF = mybir.ActivationFunctionType
OP = mybir.AluOpType
BF = ml_dtypes.bfloat16

D = 1024
H = 16
DH = 64
NF = 8
CH = 1024
FF = 4096
RANK = 64
NE = 8
NPAIR = 4
RMS_EPS = 1e-6
LN_EPS = 1e-5

SCAN_GROUPS = []
SCAN_GROUPS.append(("real", 0, 0, "zero"))
SCAN_GROUPS.append(("real", 0, 1, "carry"))
SCAN_GROUPS.append(("real", 1, 0, "carry"))
SCAN_GROUPS.append(("real", 1, 1, "carry"))
SCAN_GROUPS.append(("pfx2", 2, 0, "zero"))
SCAN_GROUPS.append(("pfx2", 2, 1, "carry"))
SCAN_GROUPS.append(("real", 2, 0, "carry"))
SCAN_GROUPS.append(("real", 2, 1, "carry"))
SCAN_GROUPS.append(("pfx3", 3, 0, "zero"))
SCAN_GROUPS.append(("pfx3", 3, 1, "carry"))
SCAN_GROUPS.append(("real", 3, 0, "carry"))
SCAN_GROUPS.append(("real", 3, 1, "carry"))


def build_program(debug_names=(), phases=3):
    nc = bass.Bass("TRN2", target_bir_lowering=False,
                   detect_race_conditions=False)

    dx = nc.dram_tensor("x", [4, CH, D], F32, kind="ExternalInput").ap()
    dxc2 = nc.dram_tensor("xc2", [CH, D], F32, kind="ExternalInput").ap()
    dxc3 = nc.dram_tensor("xc3", [CH, D], F32, kind="ExternalInput").ap()
    dq = nc.dram_tensor("qv", [H, DH], F32, kind="ExternalInput").ap()
    dk = nc.dram_tensor("kv", [H, DH], F32, kind="ExternalInput").ap()
    dld = nc.dram_tensor("logdec", [H, 1], F32, kind="ExternalInput").ap()
    dgam = nc.dram_tensor("gam", [128, NF], F32, kind="ExternalInput").ap()
    dowT = nc.dram_tensor("owT", [128, NF, D], F32, kind="ExternalInput").ap()
    dA1T = nc.dram_tensor("a1T", [128, NF, 512], F32, kind="ExternalInput").ap()
    dW1B = nc.dram_tensor("w1B", [NE * RANK, FF], F32, kind="ExternalInput").ap()
    dW2AT = nc.dram_tensor("w2AT", [NPAIR, 128, 32, 128], F32, kind="ExternalInput").ap()
    dW2B = nc.dram_tensor("w2B", [128, NPAIR, D], F32, kind="ExternalInput").ap()
    didx = nc.dram_tensor("idx", [128, 8, 2], I32, kind="ExternalInput").ap()
    dwgt = nc.dram_tensor("wgt", [128, 8, 2], F32, kind="ExternalInput").ap()
    dIf = nc.dram_tensor("If32", [128, 128], F32, kind="ExternalInput").ap()
    dIb = nc.dram_tensor("Ibf", [128, 128], BF16, kind="ExternalInput").ap()
    dSEL = nc.dram_tensor("SELf", [H, 128], BF16, kind="ExternalInput").ap()
    dM16 = nc.dram_tensor("m16x8", [H, NF], F32, kind="ExternalInput").ap()
    dPS = nc.dram_tensor("psel", [NE, NPAIR * 128], BF16, kind="ExternalInput").ap()
    dout = nc.dram_tensor("out", [4, CH, D], F32, kind="ExternalOutput").ap()
    ddbg = {}
    for nm, shape in dict(snap=[128, 5, 8], cm=[128, 8], cv=[128, 8],
                          cS=[128, 8], cs2=[128, 8], cb2=[128, 8],
                          l16=[H, 1], r3=[H, 3 * NF], selw=[H, 128],
                          u=[128, 2, NF, 512], r=[1, CH], rc=[128, 8],
                          h1=[128, 2, D], a=[128, NPAIR, 512],
                          m=[NE, 512], M=[128, NPAIR, 512],
                          cs=[128, NPAIR, 512], h2=[128, 8, D],
                          lam=[128, NF], gc=[128, NF], c2=[128, NF]).items():
        if nm in debug_names:
            ddbg[nm] = nc.dram_tensor("dbg_" + nm, shape,
                                      F32 if nm in ("r", "rc", "h1", "lam", "gc", "c2", "l16", "r3", "cm", "cv", "cS", "cs2", "cb2", "snap") else BF16,
                                      kind="ExternalOutput").ap()

    es = ExitStack()
    with es:
        E = es.enter_context
        o_wTb = E(nc.sbuf_tensor("o_wTb", [128, NF, D], BF16))
        A1b = E(nc.sbuf_tensor("A1b", [128, NF, 512], BF16))
        W2Bb = E(nc.sbuf_tensor("W2Bb", [128, NPAIR, D], BF16))
        W1Bs = E(nc.sbuf_tensor("W1Bs", [128, 2, FF], BF16))
        W2As = E(nc.sbuf_tensor("W2As", [128, 2, 32, 128], BF16))
        xfm = E(nc.sbuf_tensor("xfm", [128, 2, NF, 512], BF16))
        xb = E(nc.sbuf_tensor("xb", [128, 2, 2, 2 * D], BF16))
        ub = E(nc.sbuf_tensor("ub", [128, 2, NF, 512], BF16))
        x32 = E(nc.sbuf_tensor("x32", [128, 2, D], F32))
        sqb = E(nc.sbuf_tensor("sqb", [128, 2, NF, 512], BF16))
        h1 = E(nc.sbuf_tensor("h1", [128, 2, D], F32))
        trash = E(nc.sbuf_tensor("trash", [128, 512], BF16))
        trash2 = E(nc.sbuf_tensor("trash2", [128, 8], F32))
        outb = E(nc.sbuf_tensor("outb", [128, 2, D], F32))
        ln1o = E(nc.sbuf_tensor("ln1o", [128, 8, D], BF16))
        h2 = E(nc.sbuf_tensor("h2", [128, 8, D], BF16))
        carryT = E(nc.sbuf_tensor("carryT", [128, 2, NF], F32))
        rcolT = E(nc.sbuf_tensor("rcolT", [128, 8], F32))
        asb = E(nc.sbuf_tensor("asb", [128, NPAIR, 512], BF16))
        csb = E(nc.sbuf_tensor("csb", [128, NPAIR, 512], BF16))
        htile = E(nc.sbuf_tensor("htile", [128, 3, 1024], BF16))
        Msb = E(nc.sbuf_tensor("Msb", [128, NPAIR, 512], BF16))
        mrows = E(nc.sbuf_tensor("mrows", [NE, 512], BF16))
        mcols = E(nc.sbuf_tensor("mcols", [128, 4, NE], F32))
        idxf = E(nc.sbuf_tensor("idxf", [128, 8, 2], F32))
        wgtf = E(nc.sbuf_tensor("wgtf", [128, 8, 2], F32))
        qh = E(nc.sbuf_tensor("qh", [H, DH], F32))
        kh = E(nc.sbuf_tensor("kh", [H, DH], F32))
        ldc = E(nc.sbuf_tensor("ldc", [H, 1], F32))
        lam16 = E(nc.sbuf_tensor("lam16", [H, 1], F32))
        c16 = E(nc.sbuf_tensor("c16", [H, 1], F32))
        m16 = E(nc.sbuf_tensor("m16", [H, NF], F32))
        rhs3 = E(nc.sbuf_tensor("rhs3", [H, 3 * NF], F32))
        rhs3h = E(nc.sbuf_tensor("rhs3h", [H, 3 * NF], BF16))
        rhs3l = E(nc.sbuf_tensor("rhs3l", [H, 3 * NF], BF16))
        mch = E(nc.sbuf_tensor("mch", [128, 4, NE], BF16))
        mcl = E(nc.sbuf_tensor("mcl", [128, 4, NE], BF16))
        lamcol = E(nc.sbuf_tensor("lamcol", [128, NF], F32))
        gccol = E(nc.sbuf_tensor("gccol", [128, NF], F32))
        c2col = E(nc.sbuf_tensor("c2col", [128, NF], F32))
        gamcol = E(nc.sbuf_tensor("gamcol", [128, NF], F32))
        SELf = E(nc.sbuf_tensor("sSELf", [H, 128], BF16))
        Ibf = E(nc.sbuf_tensor("sIbf", [128, 128], BF16))
        psel = E(nc.sbuf_tensor("spsel", [NE, NPAIR * 128], BF16))
        ones1f = E(nc.sbuf_tensor("ones1f", [1, 1], F32))
        cst = E(nc.sbuf_tensor("cst", [128, 4], F32))
        onescb = E(nc.sbuf_tensor("onescb", [128, 1], BF16))
        colm = E(nc.sbuf_tensor("colm", [128, 8], F32))
        colv = E(nc.sbuf_tensor("colv", [128, 8], F32))
        cols_ = E(nc.sbuf_tensor("cols_", [128, 8], F32))
        colw = E(nc.sbuf_tensor("colw", [128, 8], F32))
        colt = E(nc.sbuf_tensor("colt", [128, 8], F32))
        cols2 = E(nc.sbuf_tensor("cols2", [128, 8], F32))
        colb2 = E(nc.sbuf_tensor("colb2", [128, 8], F32))
        colhs = E(nc.sbuf_tensor("colhs", [128, 8, 2], F32))
        colsq = E(nc.sbuf_tensor("colsq", [128, 8], F32))
        snap = E(nc.sbuf_tensor("snap", [128, 5, 8], F32))
        p3m = E(nc.sbuf_tensor("p3m", [128, 8, 2], F32))
        p3q = E(nc.sbuf_tensor("p3q", [128, 8, 2], F32))

        psA = E(nc.psum_tensor("psA", [128, D], F32))
        psB = E(nc.psum_tensor("psB", [128, D], F32))
        psR = E(nc.psum_tensor("psR", [128, D], F32))
        psY = E(nc.psum_tensor("psY", [128, D], F32))
        bview = [psA[:, 0:512], psA[:, 512:D], psB[:, 0:512], psB[:, 512:D]]

        sem_names = [
            "wgtA", "wgtB", "x32d0", "x32d1", "xbd0", "xbd1",
            "trp0", "trp1", "outd0", "outd1", "p2d0", "p2d1", "idxd",
            "s0", "prep", "scn", "sqv", "smm", "rln", "vself", "sself",
            "owt", "h1v", "vst", "ssc", "vw", "stc", "vsb", "nrm",
            "mcv", "mtm", "mev", "mp", "mpe", "amm", "aev",
            "bmm", "gel", "cmf", "cev", "ymm", "h2v",
            "v3a", "s3a", "v3b", "nrm3", "dbgs",
        ]
        sems = {n: E(nc.semaphore(n)) for n in sem_names}
        CNT = {n: 0 for n in sem_names}
        SCHED = {k: [] for k in ("sync", "gpsimd", "tensor", "vector", "scalar")}

        def emit(eng, fn):
            SCHED[eng].append(fn)

        def load_act_set(s, set_id):
            s.add_instruction(mybir.InstLoadActFuncSet(
                name=nc.get_next_instruction_name(),
                act_func_set_id=set_id, ins=[], outs=[]))
            s.drain()

        def inc(name, n=1):
            CNT[name] += n
            return CNT[name]

        VSELF = [0]
        SSELF = [0]

        def vsync(v, inst):
            VSELF[0] += 1
            inst.then_inc(sems["vself"], 1)
            v.wait_ge(sems["vself"], VSELF[0])

        def ssync(s, inst):
            SSELF[0] += 1
            inst.then_inc(sems["sself"], 1)
            s.wait_ge(sems["sself"], SSELF[0])

        def g_loads(g):
            g.dma_start(ldc[:], dld).then_inc(sems["wgtA"], 16)
            g.dma_start(Ibf[:], dIb).then_inc(sems["wgtA"], 16)
            g.dma_start(SELf[:], dSEL).then_inc(sems["wgtA"], 16)
            g.dma_start(m16[:], dM16).then_inc(sems["wgtA"], 16)
            g.dma_start(psel[:], dPS).then_inc(sems["wgtA"], 16)
            g.dma_start(qh[:], dq).then_inc(sems["wgtA"], 16)
            g.dma_start(kh[:], dk).then_inc(sems["wgtA"], 16)
            g.dma_start(gamcol[:], dgam).then_inc(sems["wgtA"], 16)
            g.dma_start(o_wTb[:], dowT).then_inc(sems["wgtB"], 16)
            g.dma_start(A1b[:], dA1T).then_inc(sems["wgtB"], 16)
            g.dma_start(W2Bb[:], dW2B).then_inc(sems["wgtB"], 16)
            g.dma_start(idxf[:], didx).then_inc(sems["idxd"], 16)
            g.dma_start(wgtf[:], dwgt).then_inc(sems["idxd"], 16)
            for gp in range(2):
                g.dma_start(W1Bs[:, gp % 2, :], dW1B[128 * gp:128 * (gp + 1), :]
                            ).then_inc(sems["p2d%d" % (gp % 2)], 16)
                g.dma_start(W2As[:, gp % 2, :, :],
                            dW2AT[gp]).then_inc(sems["p2d%d" % (gp % 2)], 16)
        emit("gpsimd", g_loads)
        WGTA_ALL = inc("wgtA", 16 * 8)
        WGTB_ALL = inc("wgtB", 16 * 3)
        inc("idxd", 32)
        P2D_AFTER = {0: inc("p2d0", 32), 1: inc("p2d1", 32)}

        def s_prep(s):
            s.wait_ge(sems["wgtA"], WGTA_ALL)
            load_act_set(s, 2)
            s.activation(lam16[:], ldc[:], AF.Sigmoid).then_inc(sems["s0"], 1)
            load_act_set(s, 6)
        emit("scalar", s_prep)
        inc("s0", 1)

        def v_prep(v):
            v.memset(ones1f[:], 1.0)
            v.memset(onescb[:], 1.0)
            v.memset(cst[:, 0:1], RMS_EPS)
            v.memset(cst[:, 1:2], LN_EPS)
            v.memset(cst[:, 2:3], 1.0 / D)
            v.memset(cst[:, 3:4], -0.5)
            v.wait_ge(sems["wgtA"], WGTA_ALL)
            v.wait_ge(sems["s0"], 1)
            _i = v.tensor_mul(trash[:].bitcast(F32)[0:H, 0:DH], qh[:], kh[:])
            vsync(v, _i)
            v.reduce_sum(c16[:], trash[:].bitcast(F32)[0:H, 0:DH],
                         axis=mybir.AxisListType.X)
            v.tensor_scalar(rhs3[:, 0:NF], m16[:], lam16[:, 0:1], None, OP.mult)
            _i = v.tensor_scalar(rhs3[:, NF:2 * NF], m16[:], c16[:, 0:1], None, OP.mult)
            vsync(v, _i)
            _i = v.tensor_scalar(rhs3[:, 2 * NF:3 * NF], rhs3[:, NF:2 * NF],
                                 c16[:, 0:1], None, OP.mult)
            vsync(v, _i)
            _i = v.tensor_copy(rhs3h[:], rhs3[:])
            vsync(v, _i)
            v.tensor_sub(rhs3l[:], rhs3[:], rhs3h[:]).then_inc(sems["prep"], 1)
        emit("vector", v_prep)
        inc("prep", 1)

        def t_prep(t):
            t.wait_ge(sems["prep"], 1)
            t.matmul(psR[:, 0:3 * NF], SELf[:], rhs3h[:], start=True, stop=False)
            t.matmul(psR[:, 0:3 * NF], SELf[:], rhs3l[:], start=False, stop=True
                     ).then_inc(sems["mtm"], 1)
        emit("tensor", t_prep)
        inc("mtm", 1)

        def v_prep2(v):
            v.wait_ge(sems["mtm"], 1)
            v.wait_ge(sems["wgtB"], 16)
            v.tensor_copy(lamcol[:], psR[:, 0:NF])
            v.tensor_copy(c2col[:], psR[:, 2 * NF:3 * NF])
            _i = v.tensor_mul(gccol[:], gamcol[:], psR[:, NF:2 * NF])
            vsync(v, _i)
            _i = v.tensor_copy(ln1o[:], o_wTb[:])
            vsync(v, _i)
            for j in range(NF):
                last = v.tensor_scalar(o_wTb[:, j, :], ln1o[:, j, :],
                                       gccol[:, j:j + 1], None, OP.mult)
            last.then_inc(sems["prep"], 1)
        emit("vector", v_prep2)
        PREP_ALL = inc("prep", 1)

        TRP_AFTER = {}
        SCN_AFTER = {}
        OWT_AFTER = {}
        SMM_AFTER = {}
        RLN_AFTER = {}
        H1V_G = {}
        H1V_AT = {}
        NRM_AT = {}
        OUTD_PAR_AT = {}
        STORES_UPTO = {}
        XBD_AFTER = {}

        real_groups = [i for i, g in enumerate(SCAN_GROUPS) if g[0] == "real"]
        gt_counter = [0]
        stores = [0]

        def emit_sec_A(gi):
            kind, pos, blk, init = SCAN_GROUPS[gi]
            par = gi % 2
            if kind == "real":
                xsrc = dx[pos]
            else:
                xsrc = dxc2 if kind == "pfx2" else dxc3
            xbs = "xbd%d" % par
            trs = "trp%d" % par
            for hlf in range(2):
                tok0 = blk * 512 + hlf * 256
                trp_need = TRP_AFTER.get(gi - 2, 0)

                def g_xb(g, xsrc=xsrc, tok0=tok0, par=par, hlf=hlf,
                         trp_need=trp_need, trs=trs, xbs=xbs):
                    if trp_need > 0:
                        g.wait_ge(sems[trs], trp_need)
                    g.dma_start(xb[:, par, hlf, :].rearrange("p (i f) -> p i f", i=2),
                                xsrc[tok0:tok0 + 256, :]
                                .rearrange("(i p) f -> p i f", p=128)
                                ).then_inc(sems[xbs], 16)
                emit("gpsimd", g_xb)
                inc(xbs, 16)
            XBD_AFTER[gi] = CNT[xbs]
            scn_need = SCN_AFTER.get(gi - 2, 0)
            for tt in range(4):
                hlf, tloc = tt // 2, tt % 2

                def s_tr(s, par=par, hlf=hlf, tloc=tloc, tt=tt,
                         xbd_n=XBD_AFTER[gi], scn_need=scn_need, trs=trs, xbs=xbs,
                         o0=CNT["outd0"], o1=CNT["outd1"]):
                    if scn_need > 0:
                        s.wait_ge(sems["scn"], scn_need)
                    s.wait_ge(sems[xbs], xbd_n)
                    if o0 > 0:
                        s.wait_ge(sems["outd0"], o0)
                    if o1 > 0:
                        s.wait_ge(sems["outd1"], o1)
                    s.dma_start(xfm[:, par, :, tt * 128:(tt + 1) * 128],
                                xb[:, par, hlf, tloc * D:(tloc + 1) * D]
                                .rearrange("p (j f) -> p j f", j=NF),
                                transpose=True).then_inc(sems[trs], 16)
                emit("scalar", s_tr)
                inc(trs, 16)
            TRP_AFTER[gi] = CNT[trs]
            owt_need = OWT_AFTER.get(gi - 2, 0)
            smm_need = SMM_AFTER.get(gi - 2, 0)

            def v_scan(v, par=par, init=init, trp_n=TRP_AFTER[gi],
                       owt_need=owt_need, smm_need=smm_need, trs=trs,
                       scn_prev=CNT["scn"], scn_base8=CNT["scn"] + 8):
                if init == "carry" and scn_prev > 0:
                    v.wait_ge(sems["scn"], scn_prev)
                if owt_need > 0:
                    v.wait_ge(sems["owt"], owt_need)
                if smm_need > 0:
                    v.wait_ge(sems["smm"], smm_need)
                v.wait_ge(sems[trs], trp_n)
                v.wait_ge(sems["prep"], PREP_ALL)
                for j in range(NF):
                    init_ap = 0.0 if init == "zero" else carryT[:, 1 - par, j:j + 1]
                    v.tensor_tensor_scan(ub[:, par, j, :],
                                         lamcol[:, j:j + 1].broadcast_to([128, 512]),
                                         xfm[:, par, j, :], init_ap,
                                         OP.mult, OP.add).then_inc(sems["scn"], 1)
                v.wait_ge(sems["scn"], scn_base8)
                v.tensor_copy(carryT[:, par, :], ub[:, par, :, 511:512]
                              ).then_inc(sems["scn"], 1)
            emit("vector", v_scan)
            inc("scn", 9)
            SCN_AFTER[gi] = CNT["scn"]

        def emit_sec_B(gi):
            sub = int(round((phases % 1) * 10)) if phases < 1 else 9
            _, pos, blk, _ = SCAN_GROUPS[gi]
            par = gi % 2
            rgidx = real_groups.index(gi)
            scn_n = SCN_AFTER[gi]
            gpar = rgidx % 2
            smm_prev = SMM_AFTER.get(real_groups[rgidx - 2], 0) if rgidx >= 2 else 0
            for j in range(NF):
                def v_sq(v, par=par, j=j, gpar=gpar, scn_n=scn_n,
                         smm_need=smm_prev if j == 0 else 0):
                    v.wait_ge(sems["scn"], scn_n)
                    if smm_need > 0:
                        v.wait_ge(sems["smm"], smm_need)
                    v.scalar_tensor_tensor(sqb[:, gpar, j, :], ub[:, par, j, :],
                                           c2col[:, j:j + 1], ub[:, par, j, :],
                                           OP.mult, OP.mult).then_inc(sems["sqv"], 1)
                emit("vector", v_sq)
                inc("sqv", 1)
            sqv_all = CNT["sqv"]
            if sub < 4:
                SMM_AFTER[gi] = CNT["smm"]
                return
            rln_need = RLN_AFTER.get(real_groups[rgidx - 1], 0) if rgidx > 0 else 0
            first_real = (rgidx == 0)

            def t_ssq(t, blk=blk, gpar=gpar, sqv_n=sqv_all, rln_need=rln_need,
                      first=first_real):
                t.wait_ge(sems["sqv"], sqv_n)
                if rln_need > 0:
                    t.wait_ge(sems["rln"], rln_need)
                if first:
                    t.wait_ge(sems["prep"], PREP_ALL)
                for tt in range(4):
                    ti = blk * 4 + tt
                    last = None
                    for j in range(NF):
                        last = t.matmul(psR[:, 504 + ti:505 + ti],
                                        sqb[:, gpar, j, tt * 128:(tt + 1) * 128],
                                        onescb[:],
                                        start=(j == 0), stop=(j == NF - 1),
                                        skip_group_check=True)
                    last.then_inc(sems["smm"], 1)
            emit("tensor", t_ssq)
            inc("smm", 4)
            SMM_AFTER[gi] = CNT["smm"]
            if sub < 5:
                return
            h1v_g_need = H1V_G.get(real_groups[rgidx - 2], 0) if rgidx >= 2 else 0

            def s_r(s, blk=blk, smm_n=SMM_AFTER[gi], h1v_g_need=h1v_g_need):
                s.wait_ge(sems["smm"], smm_n)
                if h1v_g_need > 0:
                    s.wait_ge(sems["h1v"], h1v_g_need)
                sl = slice(blk * 4, blk * 4 + 4)
                psl = slice(504 + blk * 4, 508 + blk * 4)
                _i = s.activation(rcolT[:, sl], psR[:, psl],
                                  AF.Ln, bias=cst[:, 0:1], scale=cst[:, 2:3])
                ssync(s, _i)
                s.activation(rcolT[:, sl], rcolT[:, sl],
                             AF.Exp, scale=cst[:, 3:4]).then_inc(sems["rln"], 1)
            emit("scalar", s_r)
            inc("rln", 1)
            RLN_AFTER[gi] = CNT["rln"]

        def emit_sec_C(gi):
            sub = int(round((phases % 1) * 10)) if phases < 1 else 10
            _, pos, blk, _ = SCAN_GROUPS[gi]
            par = gi % 2
            is_ffn = (pos == 3)
            for tt in range(4):
                ti = blk * 4 + tt
                gt = gt_counter[0]
                gt_counter[0] += 1
                tpar = gt % 2
                ps = psA if tpar == 0 else psB
                h1v_need = H1V_AT.get(gt - 2, 0)

                x3s = "x32d%d" % tpar

                def g_x32(g, pos=pos, ti=ti, tpar=tpar, h1v_need=h1v_need,
                          x3s=x3s):
                    if h1v_need > 0:
                        g.wait_ge(sems["h1v"], h1v_need)
                    g.dma_start(x32[:, tpar, :],
                                dx[pos, ti * 128:(ti + 1) * 128, :]
                                ).then_inc(sems[x3s], 16)
                emit("gpsimd", g_x32)
                inc(x3s, 16)
                x32_n = CNT[x3s]

                def t_ow(t, par=par, ti=ti, ps=ps, h1v_need=h1v_need):
                    if h1v_need > 0:
                        t.wait_ge(sems["h1v"], h1v_need)
                    tloc = (ti % 4) * 128
                    last = None
                    for j in range(NF):
                        for nh in range(2):
                            last = t.matmul(ps[:, nh * 512:(nh + 1) * 512],
                                            ub[:, par, j, tloc:tloc + 128],
                                            o_wTb[:, j, nh * 512:(nh + 1) * 512],
                                            start=(j == 0), stop=(j == NF - 1))
                    last.then_inc(sems["owt"], 1)
                emit("tensor", t_ow)
                inc("owt", 1)
                owt_n = CNT["owt"]
                nrm_need = NRM_AT.get(gt - 2, 0)

                def v_h1(v, ti=ti, tpar=tpar, ps=ps, owt_n=owt_n,
                         x32_n=x32_n, rln_n=CNT["rln"], nrm_need=nrm_need,
                         x3s=x3s):
                    v.wait_ge(sems["owt"], owt_n)
                    v.wait_ge(sems[x3s], x32_n)
                    v.wait_ge(sems["rln"], rln_n)
                    if nrm_need > 0 and sub >= 10:
                        v.wait_ge(sems["nrm"], nrm_need)
                    for nh in range(2):
                        _i = v.scalar_tensor_tensor(h1[:, tpar, nh * 512:(nh + 1) * 512],
                                                    ps[:, nh * 512:(nh + 1) * 512],
                                                    rcolT[:, ti:ti + 1],
                                                    x32[:, tpar, nh * 512:(nh + 1) * 512],
                                                    OP.mult, OP.add,
                                                    accum_out=colhs[:, ti, nh:nh + 1])
                        vsync(v, _i)
                    _i = v.scalar_tensor_tensor(trash[:, :], h1[:, tpar, 0:512], 1.0,
                                                h1[:, tpar, 0:512], OP.bypass, OP.mult,
                                                accum_out=colsq[:, ti:ti + 1])
                    vsync(v, _i)
                    _i = v.scalar_tensor_tensor(trash[:, :], h1[:, tpar, 512:D], 1.0,
                                                h1[:, tpar, 512:D], OP.bypass, OP.mult,
                                                accum_out=trash2[:, 1:2])
                    vsync(v, _i)
                    _i = v.tensor_add(colsq[:, ti:ti + 1], colsq[:, ti:ti + 1],
                                      trash2[:, 1:2])
                    _i2 = v.tensor_add(colm[:, ti:ti + 1], colhs[:, ti, 0:1],
                                       colhs[:, ti, 1:2])
                    vsync(v, _i2)
                    _i = v.tensor_scalar(colm[:, ti:ti + 1], colm[:, ti:ti + 1],
                                         1.0 / D, None, OP.mult)
                    vsync(v, _i)
                    _i = v.tensor_mul(trash2[:, 0:1], colm[:, ti:ti + 1],
                                      colm[:, ti:ti + 1])
                    vsync(v, _i)
                    v.scalar_tensor_tensor(colv[:, ti:ti + 1], colsq[:, ti:ti + 1],
                                           1.0 / D, trash2[:, 0:1],
                                           OP.mult, OP.subtract).then_inc(sems["h1v"], 1)
                emit("vector", v_h1)
                inc("h1v", 1)
                inc("vst", 1)
                H1V_AT[gt] = CNT["h1v"]
                vst_n = CNT["vst"]
                if sub < 9:
                    OUTD_PAR_AT[(tpar, gt)] = CNT["outd%d" % tpar]
                    for _pp in range(2):
                        if (_pp, gt) not in OUTD_PAR_AT:
                            OUTD_PAR_AT[(_pp, gt)] = OUTD_PAR_AT.get((_pp, gt - 1), 0)
                    STORES_UPTO[gt] = stores[0]
                    NRM_AT[gt] = CNT["nrm"]
                    continue

                def s_scol(s, ti=ti, vst_n=vst_n):
                    s.wait_ge(sems["h1v"], vst_n)
                    _i = s.activation(cols_[:, ti:ti + 1], colv[:, ti:ti + 1],
                                      AF.Ln, bias=cst[:, 1:2])
                    ssync(s, _i)
                    s.activation(cols_[:, ti:ti + 1], cols_[:, ti:ti + 1],
                                 AF.Exp, scale=cst[:, 3:4]).then_inc(sems["ssc"], 1)
                emit("scalar", s_scol)
                inc("ssc", 1)
                ssc_n = CNT["ssc"]

                if not is_ffn:
                    def v_w(v, ti=ti, ssc_n=ssc_n):
                        v.wait_ge(sems["ssc"], ssc_n)
                        _i = v.tensor_mul(colw[:, ti:ti + 1], cols_[:, ti:ti + 1],
                                          cols_[:, ti:ti + 1])
                        vsync(v, _i)
                        v.tensor_mul(colw[:, ti:ti + 1], colw[:, ti:ti + 1],
                                     colv[:, ti:ti + 1]).then_inc(sems["vw"], 1)
                    emit("vector", v_w)
                    inc("vw", 1)
                    vw_n = CNT["vw"]

                    def s_tcol(s, ti=ti, vw_n=vw_n):
                        s.wait_ge(sems["vw"], vw_n)
                        _i = s.activation(colt[:, ti:ti + 1], colw[:, ti:ti + 1],
                                          AF.Ln, bias=cst[:, 1:2])
                        ssync(s, _i)
                        s.activation(colt[:, ti:ti + 1], colt[:, ti:ti + 1],
                                     AF.Exp, scale=cst[:, 3:4]).then_inc(sems["stc"], 1)
                    emit("scalar", s_tcol)
                    inc("stc", 1)
                    stc_n = CNT["stc"]

                    def v_sb(v, ti=ti, stc_n=stc_n):
                        v.wait_ge(sems["stc"], stc_n)
                        _i = v.tensor_mul(cols2[:, ti:ti + 1], cols_[:, ti:ti + 1],
                                          colt[:, ti:ti + 1])
                        vsync(v, _i)
                        _i = v.tensor_mul(colb2[:, ti:ti + 1], colm[:, ti:ti + 1],
                                          cols2[:, ti:ti + 1])
                        vsync(v, _i)
                        v.tensor_scalar(colb2[:, ti:ti + 1], colb2[:, ti:ti + 1],
                                        -1.0, None, OP.mult).then_inc(sems["vsb"], 1)
                    emit("vector", v_sb)
                    inc("vsb", 1)
                else:
                    def v_sb(v, ti=ti, ssc_n=ssc_n):
                        v.wait_ge(sems["ssc"], ssc_n)
                        v.tensor_copy(cols2[:, ti:ti + 1], cols_[:, ti:ti + 1])
                        _i = v.tensor_mul(colb2[:, ti:ti + 1], colm[:, ti:ti + 1],
                                          cols_[:, ti:ti + 1])
                        vsync(v, _i)
                        v.tensor_scalar(colb2[:, ti:ti + 1], colb2[:, ti:ti + 1],
                                        -1.0, None, OP.mult).then_inc(sems["vsb"], 1)
                    emit("vector", v_sb)
                    inc("vsb", 1)
                vsb_n = CNT["vsb"]
                if sub < 10:
                    OUTD_PAR_AT[(tpar, gt)] = CNT["outd%d" % tpar]
                    for _pp in range(2):
                        if (_pp, gt) not in OUTD_PAR_AT:
                            OUTD_PAR_AT[(_pp, gt)] = OUTD_PAR_AT.get((_pp, gt - 1), 0)
                    STORES_UPTO[gt] = stores[0]
                    NRM_AT[gt] = CNT["nrm"]
                    continue
                ods = "outd%d" % tpar
                outd_need = OUTD_PAR_AT.get((tpar, gt - 2), 0)

                def s_nrm(s, ti=ti, tpar=tpar, is_ffn=is_ffn, vsb_n=vsb_n,
                          outd_need=outd_need if "nostore" not in debug_names else 0,
                          ods=ods):
                    s.wait_ge(sems["vsb"], vsb_n)
                    if is_ffn:
                        s.activation(ln1o[:, ti, :], h1[:, tpar, :], AF.Identity,
                                     bias=colb2[:, ti:ti + 1],
                                     scale=cols2[:, ti:ti + 1]).then_inc(sems["nrm"], 1)
                    else:
                        if outd_need > 0:
                            s.wait_ge(sems[ods], outd_need)
                        s.activation(outb[:, tpar, :], h1[:, tpar, :], AF.Identity,
                                     bias=colb2[:, ti:ti + 1],
                                     scale=cols2[:, ti:ti + 1]).then_inc(sems["nrm"], 1)
                emit("scalar", s_nrm)
                inc("nrm", 1)
                NRM_AT[gt] = CNT["nrm"]

                if not is_ffn and "nostore" not in debug_names:
                    def sy_out(sy, pos=pos, ti=ti, tpar=tpar, nrm_n=CNT["nrm"],
                               ods=ods, t0=CNT["trp0"], t1=CNT["trp1"]):
                        sy.wait_ge(sems["nrm"], nrm_n)
                        if t0 > 0:
                            sy.wait_ge(sems["trp0"], t0)
                        if t1 > 0:
                            sy.wait_ge(sems["trp1"], t1)
                        sy.dma_start(dout[pos, ti * 128:(ti + 1) * 128, :],
                                     outb[:, tpar, :]).then_inc(sems[ods], 16)
                    emit("sync", sy_out)
                    stores[0] += 1
                    inc(ods, 16)
                OUTD_PAR_AT[(tpar, gt)] = CNT[ods]
                for _pp in range(2):
                    if (_pp, gt) not in OUTD_PAR_AT:
                        OUTD_PAR_AT[(_pp, gt)] = OUTD_PAR_AT.get((_pp, gt - 1), CNT["outd%d" % _pp])
                STORES_UPTO[gt] = stores[0]
            OWT_AFTER[gi] = CNT["owt"]
            H1V_G[gi] = CNT["h1v"]

        def emit_snap(nrm_n):
            def v_snap(v, nrm_n=nrm_n):
                v.memset(snap[:], 0.0)
                v.wait_ge(sems["nrm"], nrm_n)
                v.tensor_copy(snap[:, 0, :], colm[:])
                v.tensor_copy(snap[:, 1, :], colv[:])
                v.tensor_copy(snap[:, 2, :], cols_[:])
                v.tensor_copy(snap[:, 3, :], cols2[:])
                v.tensor_copy(snap[:, 4, :], colb2[:]).then_inc(sems["dbgs"], 1)
            emit("vector", v_snap)

        pending_real = None
        for gi in range(len(SCAN_GROUPS)):
            emit_sec_A(gi)
            if SCAN_GROUPS[gi][0] == "real" and phases >= 0.3:
                emit_sec_B(gi)
            if pending_real is not None:
                emit_sec_C(pending_real)
                pending_real = None
            if SCAN_GROUPS[gi][0] == "real" and phases >= 0.75:
                pending_real = gi
            else:
                OWT_AFTER[gi] = CNT["owt"]
                if gi - 1 in OWT_AFTER:
                    pass
        if pending_real is not None:
            emit_sec_C(pending_real)
        if debug_names:
            emit_snap(NRM_AT.get(15, CNT["nrm"]))

        def _phase1_only():
            return phases < 2

        if phases >= 2:
            def s_gel_tab(s, nrm_n=CNT["nrm"]):
                s.wait_ge(sems["nrm"], nrm_n)
                load_act_set(s, 10)
            emit("scalar", s_gel_tab)

            GEL_AFTER = {}
            CMF_AFTER = {}
            H2V_AT = {}
            ff_g = [0]
            yunit = [0]
            CEV_AFTER = {}
            MPE_AFTER = {}

            for b2 in range(2):
                def v_m(v, b2=b2):
                    v.wait_ge(sems["idxd"], 32)
                    for ts in range(4):
                        tix = b2 * 4 + ts
                        for e in range(NE):
                            _i = v.scalar_tensor_tensor(trash2[:, 0:2], idxf[:, tix, :],
                                                        float(e), wgtf[:, tix, :],
                                                        OP.is_equal, OP.mult,
                                                        accum_out=mcols[:, ts, e:e + 1])
                            vsync(v, _i)
                    _i = v.tensor_copy(mch[:], mcols[:])
                    vsync(v, _i)
                    v.tensor_sub(mcl[:], mcols[:], mch[:]).then_inc(sems["mcv"], 1)
                emit("vector", v_m)
                inc("mcv", 1)

                def t_mt(t, mcv_n=CNT["mcv"], rln_n=CNT["rln"],
                         cev_n=CEV_AFTER.get((b2 - 1, NPAIR - 1), 0)):
                    t.wait_ge(sems["mcv"], mcv_n)
                    t.wait_ge(sems["rln"], rln_n)
                    if cev_n > 0:
                        t.wait_ge(sems["cev"], cev_n)
                    last = None
                    for ts in range(4):
                        t.matmul(psR[0:NE, ts * 128:(ts + 1) * 128],
                                 mch[:, ts, :], Ibf[:], start=True, stop=False)
                        last = t.matmul(psR[0:NE, ts * 128:(ts + 1) * 128],
                                        mcl[:, ts, :], Ibf[:], start=False, stop=True)
                    last.then_inc(sems["mtm"], 1)
                emit("tensor", t_mt)
                inc("mtm", 1)

                def v_mev(v, mtm_n=CNT["mtm"]):
                    v.wait_ge(sems["mtm"], mtm_n)
                    v.tensor_copy(mrows[:, :], psR[0:NE, 0:512]).then_inc(sems["mev"], 1)
                emit("vector", v_mev)
                inc("mev", 1)

                mpe_before = CNT["mpe"]

                def t_mp(t, mev_n=CNT["mev"], mpe_before=mpe_before):
                    t.wait_ge(sems["mev"], mev_n)
                    for p in range(NPAIR):
                        if p >= 2:
                            t.wait_ge(sems["mpe"], mpe_before + p - 1)
                        t.matmul(psR[:, (p % 2) * 512:(p % 2 + 1) * 512],
                                 psel[:, p * 128:(p + 1) * 128], mrows[:, :],
                                 start=True, stop=True).then_inc(sems["mp"], 1)
                emit("tensor", t_mp)
                mp_base = CNT["mp"]
                inc("mp", NPAIR)

                def v_mpe(v, mp_base=mp_base):
                    for p in range(NPAIR):
                        v.wait_ge(sems["mp"], mp_base + p + 1)
                        v.tensor_copy(Msb[:, p, :],
                                      psR[:, (p % 2) * 512:(p % 2 + 1) * 512]).then_inc(sems["mpe"], 1)
                emit("vector", v_mpe)
                inc("mpe", NPAIR)
                MPE_AFTER[b2] = CNT["mpe"]

                for tt in range(4):
                    tix = b2 * 4 + tt

                    def s_xf(s, tix=tix, tt=tt, b2=b2, nrm_n=CNT["nrm"],
                             scn_n=CNT["scn"], o0=CNT["outd0"], o1=CNT["outd1"]):
                        s.wait_ge(sems["nrm"], nrm_n)
                        s.wait_ge(sems["scn"], scn_n)
                        if o0 > 0:
                            s.wait_ge(sems["outd0"], o0)
                        if o1 > 0:
                            s.wait_ge(sems["outd1"], o1)
                        s.dma_start(xfm[:, b2, :, tt * 128:(tt + 1) * 128],
                                    ln1o[:, tix, :].rearrange("p (j f) -> p j f", j=NF),
                                    transpose=True).then_inc(sems["trp%d" % b2], 16)
                    emit("scalar", s_xf)
                    inc("trp%d" % b2, 16)
                trp_n2 = CNT["trp%d" % b2]

                AEV_AFTER = {}
                for rs in range(NPAIR):
                    aev_need = AEV_AFTER.get(rs - 2, 0)

                    def t_a(t, rs=rs, b2=b2, trp_n=trp_n2, mpe_n=MPE_AFTER[b2],
                            aev_need=aev_need):
                        t.wait_ge(sems["trp%d" % b2], trp_n)
                        t.wait_ge(sems["mpe"], mpe_n)
                        if aev_need > 0:
                            t.wait_ge(sems["aev"], aev_need)
                        last = None
                        for j in range(NF):
                            last = t.matmul(psR[:, (rs % 2) * 512:(rs % 2 + 1) * 512],
                                            A1b[:, j, rs * 128:(rs + 1) * 128],
                                            xfm[:, b2, j, :],
                                            start=(j == 0), stop=(j == NF - 1))
                        last.then_inc(sems["amm"], 1)
                    emit("tensor", t_a)
                    inc("amm", 1)

                    def v_aev(v, rs=rs, amm_n=CNT["amm"]):
                        v.wait_ge(sems["amm"], amm_n)
                        v.tensor_copy(asb[:, rs, :],
                                      psR[:, (rs % 2) * 512:(rs % 2 + 1) * 512]).then_inc(sems["aev"], 1)
                    emit("vector", v_aev)
                    inc("aev", 1)
                    AEV_AFTER[rs] = CNT["aev"]
                aev_all = CNT["aev"]

                for p in range(NPAIR):
                    gp = b2 * NPAIR + p
                    cev_need = CEV_AFTER.get((b2, p - 2), 0)
                    for ffi in range(32):
                        fg = ff_g[0]
                        ff_g[0] += 1
                        bb0 = (2 * fg) % 4
                        bb1 = (2 * fg + 1) % 4
                        gel_need = GEL_AFTER.get(fg - 2, 0)

                        def t_b(t, p=p, gp=gp, ffi=ffi, bb0=bb0, bb1=bb1,
                                aev_n=aev_all, p2d_n=P2D_AFTER[gp],
                                gel_need=gel_need):
                            t.wait_ge(sems["aev"], aev_n)
                            t.wait_ge(sems["p2d%d" % (gp % 2)], p2d_n)
                            if gel_need > 0:
                                t.wait_ge(sems["gel"], gel_need)
                            t.matmul(bview[bb0],
                                     W1Bs[0:64, gp % 2, ffi * 128:(ffi + 1) * 128],
                                     asb[0:64, p, :], start=True, stop=True,
                                     tile_position=(0, 0))
                            t.matmul(bview[bb1],
                                     W1Bs[64:128, gp % 2, ffi * 128:(ffi + 1) * 128],
                                     asb[64:128, p, :], start=True, stop=True,
                                     tile_position=(64, 0)).then_inc(sems["bmm"], 1)
                        emit("tensor", t_b)
                        inc("bmm", 1)
                        bmm_n = CNT["bmm"]
                        cmf_need = CMF_AFTER.get(fg - 3, 0)

                        def s_g(s, ffi=ffi, fg=fg, bb0=bb0, bb1=bb1, bmm_n=bmm_n,
                                cmf_need=cmf_need):
                            s.wait_ge(sems["bmm"], bmm_n)
                            if cmf_need > 0:
                                s.wait_ge(sems["cmf"], cmf_need)
                            s.activation(htile[:, fg % 3, 0:512],
                                         bview[bb0], AF.Gelu)
                            s.activation(htile[:, fg % 3, 512:1024],
                                         bview[bb1], AF.Gelu).then_inc(sems["gel"], 1)
                        emit("scalar", s_g)
                        inc("gel", 1)
                        GEL_AFTER[fg] = CNT["gel"]
                        gel_n = CNT["gel"]

                        def t_c(t, p=p, gp=gp, ffi=ffi, fg=fg, gel_n=gel_n,
                                cev_need=cev_need if ffi == 0 else 0):
                            t.wait_ge(sems["gel"], gel_n)
                            if cev_need > 0:
                                t.wait_ge(sems["cev"], cev_need)
                            t.matmul(psR[0:64, (p % 2) * 512:(p % 2 + 1) * 512],
                                     W2As[:, gp % 2, ffi, 0:64],
                                     htile[:, fg % 3, 0:512],
                                     start=(ffi == 0), stop=(ffi == 31),
                                     tile_position=(0, 0), skip_group_check=True)
                            t.matmul(psR[64:128, (p % 2) * 512:(p % 2 + 1) * 512],
                                     W2As[:, gp % 2, ffi, 64:128],
                                     htile[:, fg % 3, 512:1024],
                                     start=(ffi == 0), stop=(ffi == 31),
                                     tile_position=(0, 64), skip_group_check=True
                                     ).then_inc(sems["cmf"], 1)
                        emit("tensor", t_c)
                        inc("cmf", 1)
                        CMF_AFTER[fg] = CNT["cmf"]

                    def v_cev(v, p=p, b2=b2, cmf_n=CNT["cmf"], mpe_n=MPE_AFTER[b2]):
                        v.wait_ge(sems["cmf"], cmf_n)
                        v.wait_ge(sems["mpe"], mpe_n)
                        v.tensor_mul(csb[:, p, :],
                                     psR[:, (p % 2) * 512:(p % 2 + 1) * 512],
                                     Msb[:, p, :]).then_inc(sems["cev"], 1)
                    emit("vector", v_cev)
                    inc("cev", 1)
                    CEV_AFTER[(b2, p)] = CNT["cev"]

                    if gp + 2 < 8:
                        gpn = gp + 2

                        def g_w2(g, gpn=gpn, cmf_n=CNT["cmf"]):
                            g.wait_ge(sems["cmf"], cmf_n)
                            pn = gpn % NPAIR
                            g.dma_start(W1Bs[:, gpn % 2, :],
                                        dW1B[128 * pn:128 * (pn + 1), :]
                                        ).then_inc(sems["p2d%d" % (gpn % 2)], 16)
                            g.dma_start(W2As[:, gpn % 2, :, :],
                                        dW2AT[pn]).then_inc(sems["p2d%d" % (gpn % 2)], 16)
                        emit("gpsimd", g_w2)
                        P2D_AFTER[gpn] = inc("p2d%d" % (gpn % 2), 32)

                cev_all = CEV_AFTER[(b2, NPAIR - 1)]
                for ts in range(4):
                    tix = b2 * 4 + ts
                    for jh in range(2):
                        u = yunit[0]
                        yunit[0] += 1
                        ybank = u % 2
                        h2v_need = H2V_AT.get(u - 2, 0)

                        def t_y(t, tix=tix, ts=ts, jh=jh, ybank=ybank,
                                cev_n=cev_all, h2v_need=h2v_need, wg=WGTB_ALL):
                            t.wait_ge(sems["cev"], cev_n)
                            t.wait_ge(sems["wgtB"], wg)
                            if h2v_need > 0:
                                t.wait_ge(sems["h2v"], h2v_need)
                            yv = psY[:, ybank * 512:(ybank + 1) * 512]
                            t.matmul(yv, Ibf[:],
                                     ln1o[:, tix, jh * 512:(jh + 1) * 512],
                                     start=True, stop=False)
                            last = None
                            for p_ in range(NPAIR):
                                last = t.matmul(yv,
                                                csb[:, p_, ts * 128:(ts + 1) * 128],
                                                W2Bb[:, p_, jh * 512:(jh + 1) * 512],
                                                start=False, stop=(p_ == NPAIR - 1))
                            last.then_inc(sems["ymm"], 1)
                        emit("tensor", t_y)
                        inc("ymm", 1)

                        def v_h2(v, tix=tix, jh=jh, ybank=ybank, ymm_n=CNT["ymm"],
                                 h2v_self=CNT["h2v"]):
                            v.wait_ge(sems["ymm"], ymm_n)
                            if h2v_self > 0:
                                v.wait_ge(sems["h2v"], h2v_self)
                            yv = psY[:, ybank * 512:(ybank + 1) * 512]
                            _i = v.tensor_scalar(h2[:, tix, jh * 512:(jh + 1) * 512],
                                                 yv, 1.0, 0.0, OP.mult, OP.add,
                                                 accum_out=p3m[:, tix, jh:jh + 1])
                            vsync(v, _i)
                            v.scalar_tensor_tensor(trash[:, 0:512],
                                                   h2[:, tix, jh * 512:(jh + 1) * 512],
                                                   1.0,
                                                   h2[:, tix, jh * 512:(jh + 1) * 512],
                                                   OP.bypass, OP.mult,
                                                   accum_out=p3q[:, tix, jh:jh + 1]).then_inc(sems["h2v"], 1)
                        emit("vector", v_h2)
                        inc("h2v", 1)
                        H2V_AT[u] = CNT["h2v"]

            def s_p3_tab(s, gel_n=CNT["gel"]):
                s.wait_ge(sems["gel"], gel_n)
                load_act_set(s, 6)
            emit("scalar", s_p3_tab)

            P3_OUTD_AT = {}
            P3_OUTD_AT[-2] = CNT["outd0"]
            P3_OUTD_AT[-1] = CNT["outd1"]
            for tix in range(8):
                h2v_need = H2V_AT[tix * 2 + 1]

                def v_3a(v, tix=tix, h2v_need=h2v_need):
                    v.wait_ge(sems["h2v"], h2v_need)
                    _i = v.tensor_add(colm[:, tix:tix + 1], p3m[:, tix, 0:1],
                                      p3m[:, tix, 1:2])
                    vsync(v, _i)
                    _i = v.tensor_scalar(colm[:, tix:tix + 1], colm[:, tix:tix + 1],
                                         1.0 / D, None, OP.mult)
                    v.tensor_add(colsq[:, tix:tix + 1], p3q[:, tix, 0:1], p3q[:, tix, 1:2])
                    vsync(v, _i)
                    _i = v.tensor_mul(trash2[:, 0:1], colm[:, tix:tix + 1],
                                      colm[:, tix:tix + 1])
                    vsync(v, _i)
                    v.scalar_tensor_tensor(colv[:, tix:tix + 1], colsq[:, tix:tix + 1],
                                           1.0 / D, trash2[:, 0:1], OP.mult, OP.subtract).then_inc(sems["v3a"], 1)
                emit("vector", v_3a)
                inc("v3a", 1)

                def s_3a(s, tix=tix, v3a_n=CNT["v3a"]):
                    s.wait_ge(sems["v3a"], v3a_n)
                    s.activation(cols_[:, tix:tix + 1], colv[:, tix:tix + 1],
                                 AF.Ln, bias=cst[:, 1:2])
                    s.activation(cols_[:, tix:tix + 1], cols_[:, tix:tix + 1],
                                 AF.Exp, scale=cst[:, 3:4]).then_inc(sems["s3a"], 1)
                emit("scalar", s_3a)
                inc("s3a", 1)

                def v_3b(v, tix=tix, s3a_n=CNT["s3a"]):
                    v.wait_ge(sems["s3a"], s3a_n)
                    _i = v.tensor_mul(colb2[:, tix:tix + 1], colm[:, tix:tix + 1],
                                      cols_[:, tix:tix + 1])
                    vsync(v, _i)
                    v.tensor_scalar(colb2[:, tix:tix + 1], colb2[:, tix:tix + 1],
                                    -1.0, None, OP.mult).then_inc(sems["v3b"], 1)
                emit("vector", v_3b)
                inc("v3b", 1)
                ods3 = "outd%d" % (tix % 2)
                outd_need3 = P3_OUTD_AT[tix - 2]

                def s_3n(s, tix=tix, v3b_n=CNT["v3b"], outd_need=outd_need3,
                         ods3=ods3):
                    s.wait_ge(sems["v3b"], v3b_n)
                    if outd_need > 0:
                        s.wait_ge(sems[ods3], outd_need)
                    s.activation(outb[:, tix % 2, :], h2[:, tix, :], AF.Identity,
                                 bias=colb2[:, tix:tix + 1], scale=cols_[:, tix:tix + 1]).then_inc(sems["nrm3"], 1)
                emit("scalar", s_3n)
                inc("nrm3", 1)

                def sy_3o(sy, tix=tix, nrm3_n=CNT["nrm3"], ods3=ods3):
                    sy.wait_ge(sems["nrm3"], nrm3_n)
                    sy.dma_start(dout[3, tix * 128:(tix + 1) * 128, :],
                                 outb[:, tix % 2, :]).then_inc(sems[ods3], 16)
                emit("sync", sy_3o)
                stores[0] += 1
                inc(ods3, 16)
                P3_OUTD_AT[tix] = CNT[ods3]

        if ddbg:
            def sy_dbg(sy, targets=dict(ddbg), o0=CNT["outd0"], o1=CNT["outd1"],
                       want_snap=("snap" in ddbg)):
                sy.wait_ge(sems["outd0"], o0)
                sy.wait_ge(sems["outd1"], o1)
                if want_snap:
                    sy.wait_ge(sems["dbgs"], 1)
                nd = 0
                for nm, ap in targets.items():
                    src = {"snap": snap, "cm": colm, "cv": colv, "cS": cols_, "cs2": cols2,
                           "cb2": colb2,
                           "l16": lam16, "r3": rhs3, "selw": SELf,
                           "u": ub, "rc": rcolT, "h1": h1, "a": asb,
                           "m": mrows, "M": Msb, "cs": csb, "h2": h2,
                           "lam": lamcol, "gc": gccol, "c2": c2col}[nm]
                    sy.dma_start(ap, src[:]).then_inc(sems["dbgs"], 16)
                    nd += 1
                sy.wait_ge(sems["dbgs"], 16 * nd + (1 if want_snap else 0))
            emit("sync", sy_dbg)

        def sy_fin(sy, o0=CNT["outd0"], o1=CNT["outd1"]):
            sy.wait_ge(sems["outd0"], o0)
            sy.wait_ge(sems["outd1"], o1)
        emit("sync", sy_fin)

        block = E(nc.Block())

        @block.gpsimd
        def _(g):
            for fn in SCHED["gpsimd"]:
                fn(g)

        @block.scalar
        def _(s):
            for fn in SCHED["scalar"]:
                fn(s)

        @block.tensor
        def _(t):
            for fn in SCHED["tensor"]:
                fn(t)

        @block.vector
        def _(v):
            for fn in SCHED["vector"]:
                fn(v)

        @block.sync
        def _(sy):
            for fn in SCHED["sync"]:
                fn(sy)

    return nc


def _core_chunks(c):
    return [2 * (4 + c), 2 * (4 + c) + 1,
            2 * (12 + c // 2) + ((c + 1) % 2),
            c]


def _consts():
    SELf = np.zeros((H, 128), dtype=np.float32)
    for h in range(H):
        for p in range(128):
            if h % 2 == p // 64:
                SELf[h, p] = 1.0
    m16 = np.zeros((H, NF), dtype=np.float32)
    for h in range(H):
        m16[h, h // 2] = 1.0
    psel = np.zeros((NE, NPAIR * 128), dtype=np.float32)
    for p in range(NPAIR):
        psel[2 * p, p * 128:p * 128 + 64] = 1.0
        psel[2 * p + 1, p * 128 + 64:p * 128 + 128] = 1.0
    return (np.eye(128, dtype=np.float32),
            np.eye(128, dtype=np.float32).astype(BF),
            SELf, m16, psel.astype(BF))


def kernel(L0_attn_x, L0_attn_q, L0_attn_k, L0_attn_log_decay,
           L0_attn_norm_weight, L0_attn_o_w, L0_ffn_expert_weights,
           L0_ffn_expert_indices, L0_ffn_w1_A, L0_ffn_w1_B, L0_ffn_w2_A,
           L0_ffn_w2_B):
    x = np.ascontiguousarray(np.asarray(L0_attn_x, dtype=np.float32)
                             ).reshape(32 * CH, D)
    owT = np.ascontiguousarray(np.asarray(L0_attn_o_w, dtype=np.float32).T
                               .reshape(NF, 128, D).transpose(1, 0, 2))
    a1T = np.ascontiguousarray(np.asarray(L0_ffn_w1_A, dtype=np.float32).T
                               .reshape(NF, 128, 512).transpose(1, 0, 2))
    w1B = np.ascontiguousarray(np.asarray(L0_ffn_w1_B, dtype=np.float32))
    _w2at = np.asarray(L0_ffn_w2_A, dtype=np.float32).T
    w2AT = np.ascontiguousarray(np.stack([
        _w2at[:, 128 * p:128 * (p + 1)].reshape(32, 128, 128).transpose(1, 0, 2)
        for p in range(NPAIR)]))
    w2B = np.ascontiguousarray(np.asarray(L0_ffn_w2_B, dtype=np.float32)
                               .reshape(NPAIR, 128, D).transpose(1, 0, 2))
    qv = np.ascontiguousarray(np.asarray(L0_attn_q, dtype=np.float32).reshape(H, DH))
    kv = np.ascontiguousarray(np.asarray(L0_attn_k, dtype=np.float32).reshape(H, DH))
    ld = np.ascontiguousarray(np.asarray(L0_attn_log_decay,
                                         dtype=np.float32).reshape(H, 1))
    gam = np.ascontiguousarray(np.asarray(L0_attn_norm_weight, dtype=np.float32).reshape(NF, 128).T)
    eidx = np.ascontiguousarray(np.asarray(L0_ffn_expert_indices, dtype=np.int32))
    ewgt = np.ascontiguousarray(np.asarray(L0_ffn_expert_weights, dtype=np.float32))
    If, Ib, SELf, m16, psel = _consts()
    zeros_ch = np.zeros((CH, D), dtype=np.float32)

    nc = build_program()
    in_maps = []
    for c in range(8):
        chunks = _core_chunks(c)
        xo = np.stack([x[k * CH:(k + 1) * CH, :] for k in chunks])
        def prefix(k):
            b, hlf = k // 2, k % 2
            if hlf == 1:
                return np.ascontiguousarray(x[b * 2 * CH:(b * 2 + 1) * CH, :])
            return zeros_ch
        in_maps.append({
            "x": xo, "xc2": prefix(chunks[2]), "xc3": prefix(chunks[3]),
            "qv": qv, "kv": kv, "logdec": ld, "gam": gam, "owT": owT,
            "a1T": a1T, "w1B": w1B, "w2AT": w2AT, "w2B": w2B,
            "idx": np.ascontiguousarray(eidx[c * CH:(c + 1) * CH, :]
                                        .reshape(8, 128, 2).transpose(1, 0, 2)),
            "wgt": np.ascontiguousarray(ewgt[c * CH:(c + 1) * CH, :]
                                        .reshape(8, 128, 2).transpose(1, 0, 2)),
            "If32": If, "Ibf": Ib, "SELf": SELf.astype(BF), "m16x8": m16, "psel": psel,
        })
    res = run_bass_kernel_spmd(nc, in_maps, core_ids=list(range(8)))
    out = np.empty((32 * CH, D), dtype=np.float32)
    for c in range(8):
        oc = res.results[c]["out"]
        for pos, k in enumerate(_core_chunks(c)):
            out[k * CH:(k + 1) * CH, :] = oc[pos]
    return out.reshape(16, 2048, D)
